# version 09
# baseline (speedup 1.0000x reference)
import contextlib
import numpy as np

import concourse.bass as bass
import concourse.mybir as mybir
from concourse.bass_utils import run_bass_kernel_spmd
from concourse.alu_op_type import AluOpType as OP

F32 = mybir.dt.float32
AF = mybir.ActivationFunctionType
AX = mybir.AxisListType

ENGINES = ("tensor", "vector", "scalar", "gpsimd", "sync")

B, L, D = 2, 4096, 2048
H, HD = 8, 256
NCORE = 8
OWN = 1024
NQC = 4
KEXT = [8, 16, 24, 32]
NMASK = 8
LN_EPS = 1e-5
RMS_EPS = 1e-6

POOL = 52224
KB = 0
VB = 8192
QB = 16384
WKS = 16384
WVS = 20480
XF = 24576
XB = 32768
XSQ = 40960
XNT = 43008
KRW = 47104
KSQ = 48128
CKS = 49152
SKS = 49664
RWB1 = 50176
XNH = 32768
WQS = 40960
QRW = 45056
QS2 = 46080
CQF = 47104
SQF = 48128
RWB2 = 49152
CTX = 32768
PRB = 36864
MSK = 37888
XQT = 39936
RBS = 44032
OTB = 44544
WOS = 45568
ROW3 = 49664
HFM = 0
HTM = 16384
XNQ = 32768
WGU = 36864
WDS = 40960
SCR = 45056
RBT = 50176
OTF = 50688


class Prog:
    def __init__(self):
        self.ops = {e: [] for e in ENGINES}
        self.cnt = {}

    def emit(self, eng, opname, *args, waits=(), inc=None, incn=1, **kwargs):
        ws = [(s, v) for (s, v) in waits if v > 0]
        self.ops[eng].append((opname, args, kwargs, ws, inc, incn))
        if inc is not None:
            self.cnt[inc] = self.cnt.get(inc, 0) + incn
        return self.cnt.get(inc, 0)

    def c(self, sem):
        return self.cnt.get(sem, 0)

    def barrier(self):
        self.emit("gpsimd", "nop",
                  waits=[("sA", self.c("sA")), ("sG", self.c("sG")),
                         ("sR", self.c("sR"))], inc="bar")
        self.emit("sync", "nop", waits=[("sB", self.c("sB"))], inc="bar")
        self.emit("vector", "drain")
        self.emit("scalar", "drain")
        for e in ("tensor", "vector", "scalar"):
            self.emit(e, "nop", inc="bar")
        tgt = self.c("bar")
        for e in ENGINES:
            self.emit(e, "nop", waits=[("bar", tgt)])


def build_program(phases=4, dump=False):
    nc = bass.Bass("TRN2", target_bir_lowering=False, debug=False,
                   detect_race_conditions=False)

    d_xs = nc.dram_tensor("xs", [L, D], F32, kind="ExternalInput").ap()
    d_xq = nc.dram_tensor("xq", [OWN, D], F32, kind="ExternalInput").ap()
    d_ck = nc.dram_tensor("ckf", [128, L], F32, kind="ExternalInput").ap()
    d_sk = nc.dram_tensor("skf", [128, L], F32, kind="ExternalInput").ap()
    d_cq = nc.dram_tensor("cqf", [128, OWN], F32, kind="ExternalInput").ap()
    d_sq = nc.dram_tensor("sqf", [128, OWN], F32, kind="ExternalInput").ap()
    d_mask = nc.dram_tensor("mask", [NQC, 128, NMASK, 256], F32, kind="ExternalInput").ap()
    d_wq = nc.dram_tensor("wq", [16, 128, 16, 128], F32, kind="ExternalInput").ap()
    d_wk = nc.dram_tensor("wk", [128, 2, 16, 128], F32, kind="ExternalInput").ap()
    d_wv = nc.dram_tensor("wv", [128, 16, 256], F32, kind="ExternalInput").ap()
    d_wo = nc.dram_tensor("wo", [16, 128, 2048], F32, kind="ExternalInput").ap()
    d_wgu = nc.dram_tensor("wgu", [16, 8, 128, 2, 1024], F32, kind="ExternalInput").ap()
    d_wd = nc.dram_tensor("wd", [64, 128, 2048], F32, kind="ExternalInput").ap()
    d_gq = nc.dram_tensor("gq", [128, 16], F32, kind="ExternalInput").ap()
    d_gk = nc.dram_tensor("gk", [128, 2], F32, kind="ExternalInput").ap()
    d_id = nc.dram_tensor("ident", [128, 128], F32, kind="ExternalInput").ap()
    d_out = nc.dram_tensor("out", [OWN, D], F32, kind="ExternalOutput").ap()
    d_xnsp = nc.dram_tensor("xnsp", [8, 128, D], F32)

    es = contextlib.ExitStack()
    F = es.enter_context(nc.sbuf_tensor("F", [128, POOL], F32)).ap()
    IDN = es.enter_context(nc.sbuf_tensor("IDN", [128, 128], F32)).ap()
    ONE = es.enter_context(nc.sbuf_tensor("ONE", [128, 1], F32)).ap()
    ONR = es.enter_context(nc.sbuf_tensor("ONR", [1, 128], F32)).ap()
    GQ = es.enter_context(nc.sbuf_tensor("GQ", [128, 16], F32)).ap()
    GK = es.enter_context(nc.sbuf_tensor("GK", [128, 2], F32)).ap()
    ST = es.enter_context(nc.sbuf_tensor("ST", [128, 52], F32)).ap()
    PS = es.enter_context(nc.psum_tensor("PS", [128, 4096], F32)).ap()
    sems = {}
    for s in ("sA", "sB", "sC", "sD", "sE", "sG", "sR", "bar"):
        sems[s] = es.enter_context(nc.semaphore(s))

    p = Prog()

    def bank(b, n=512):
        return PS[:, b * 512:b * 512 + n]

    def bankrow(b, n=512):
        return PS[0:1, b * 512:b * 512 + n]

    def fsl(off, n):
        return F[:, off:off + n]

    def mm(out_ap, lhsT, rhs, start, stop, waits=(), inc=None):
        return p.emit("tensor", "matmul", out_ap, lhsT, rhs, start=start,
                      stop=stop, waits=waits, inc=inc)

    def vdrain():
        p.emit("vector", "drain")

    def rsqrt_cols(vc, wv):
        p.emit("scalar", "activation", ST[:, 6:7], ST[:, vc:vc + 1], AF.Sqrt,
               waits=wv, inc="sD")
        wd = [("sD", p.c("sD"))]
        p.emit("vector", "reciprocal", ST[:, 7:8], ST[:, 6:7], waits=wd)
        vdrain()
        p.emit("vector", "tensor_tensor", ST[:, 8:9], ST[:, 7:8], ST[:, 7:8], op=OP.mult)
        vdrain()
        p.emit("vector", "tensor_tensor", ST[:, 9:10], ST[:, 8:9], ST[:, vc:vc + 1], op=OP.mult)
        vdrain()
        p.emit("vector", "tensor_scalar", ST[:, 10:11], ST[:, 9:10], -0.5, 1.5,
               op0=OP.mult, op1=OP.add)
        vdrain()
        p.emit("vector", "tensor_tensor", ST[:, 11:12], ST[:, 7:8], ST[:, 10:11], op=OP.mult)

    def rsqrt_row(src_row, n, scale, eps, wout, base):
        veps = F[0:1, base + 512:base + 512 + n]
        Ar = F[0:1, base + 1024:base + 1024 + n]
        Br = F[0:1, base + 1536:base + 1536 + n]
        out = F[0:1, base:base + n]
        e = p.emit("vector", "tensor_scalar", veps, src_row, scale, eps,
                   op0=OP.mult, op1=OP.add, waits=[("sC", p.c("sC"))], inc="sE")
        p.emit("scalar", "activation", Ar, veps, AF.Sqrt,
               waits=[("sE", e)], inc="sD")
        p.emit("vector", "reciprocal", Br, Ar, waits=[("sD", p.c("sD"))])
        vdrain()
        p.emit("vector", "tensor_tensor", Ar, Br, Br, op=OP.mult)
        vdrain()
        p.emit("vector", "tensor_tensor", out, Ar, veps, op=OP.mult, waits=wout)
        vdrain()
        p.emit("vector", "tensor_scalar", Ar, out, -0.5, 1.5, op0=OP.mult, op1=OP.add)
        vdrain()
        p.emit("vector", "tensor_tensor", out, Br, Ar, op=OP.mult, inc="sE")
        return e

    p.emit("gpsimd", "dma_start", out=IDN[:, :], in_=d_id[:, :], inc="sA", incn=16)
    p.emit("gpsimd", "dma_start", out=GQ[:, :], in_=d_gq[:, :], inc="sA", incn=16)
    p.emit("gpsimd", "dma_start", out=GK[:, :], in_=d_gk[:, :], inc="sA", incn=16)
    p.emit("sync", "dma_start",
           out=fsl(WKS, 4096).rearrange("p (a b) -> p a b", a=32),
           in_=d_wk.rearrange("p fc dc f -> p (fc dc) f"), inc="sB", incn=16)
    p.emit("sync", "dma_start",
           out=fsl(WVS, 4096).rearrange("p (a b) -> p a b", a=16),
           in_=d_wv[:, :, :], inc="sB", incn=16)
    p.emit("vector", "memset", ONE[:, :], 1.0)
    p.emit("vector", "memset", ONR[:, :], 1.0, inc="sE")
    const_sE = p.c("sE")
    const_sA = p.c("sA")
    const_sB = p.c("sB")

    xslot_free = [("sE", 0)] * 4
    xnt_free = [("sC", 0), ("sC", 0)]
    tbank_free = [("sD", 0), ("sD", 0)]
    kbank_free = [("sE", 0), ("sE", 0)]
    vbank_free = [("sD", 0), ("sD", 0)]
    cks_free = ("sE", 0)
    row4_free = ("sE", 0)
    krw_war = ("sD", 0)
    bc5_war = ("sC", 0)

    def xn_view(dcg, ci):
        v = fsl(XF, 8192).rearrange("p (dc t) -> p dc t", dc=16)
        return v[:, 4 * dcg:4 * dcg + 4, ci * 128:(ci + 1) * 128]

    def ln_tile(srcs):
        amarks = []
        for c in range(4):
            a = p.emit("gpsimd", "dma_start", out=fsl(XB + c * 2048, 2048),
                       in_=srcs[c], waits=[xslot_free[c], ("sA", const_sA)],
                       inc="sA", incn=16)
            amarks.append(a)
        for c in range(4):
            p.emit("vector", "tensor_reduce", ST[:, c:c + 1],
                   fsl(XB + c * 2048, 2048), axis=AX.X, op=OP.add,
                   waits=[("sA", amarks[c])])
        for c in range(4):
            p.emit("scalar", "activation", fsl(XSQ, 2048), fsl(XB + c * 2048, 2048),
                   AF.Square, accum_out=ST[:, 4 + c:5 + c],
                   waits=[("sA", amarks[c])],
                   inc="sD" if c == 3 else None)
        wsq = [("sD", p.c("sD"))]
        vdrain()
        p.emit("vector", "tensor_scalar", ST[:, 8:12], ST[:, 0:4], 1.0 / D, None,
               op0=OP.mult)
        p.emit("vector", "tensor_scalar", ST[:, 12:16], ST[:, 4:8], 1.0 / D, LN_EPS,
               op0=OP.mult, op1=OP.add, waits=wsq)
        vdrain()
        p.emit("vector", "tensor_tensor", ST[:, 16:20], ST[:, 8:12], ST[:, 8:12],
               op=OP.mult)
        vdrain()
        p.emit("vector", "tensor_tensor", ST[:, 20:24], ST[:, 12:16], ST[:, 16:20],
               op=OP.subtract, inc="sE")
        p.emit("scalar", "activation", ST[:, 24:28], ST[:, 20:24], AF.Sqrt,
               waits=[("sE", p.c("sE"))], inc="sD")
        p.emit("vector", "reciprocal", ST[:, 28:32], ST[:, 24:28],
               waits=[("sD", p.c("sD"))])
        vdrain()
        p.emit("vector", "tensor_tensor", ST[:, 32:36], ST[:, 28:32], ST[:, 28:32],
               op=OP.mult)
        vdrain()
        p.emit("vector", "tensor_tensor", ST[:, 36:40], ST[:, 32:36], ST[:, 20:24],
               op=OP.mult)
        vdrain()
        p.emit("vector", "tensor_scalar", ST[:, 40:44], ST[:, 36:40], -0.5, 1.5,
               op0=OP.mult, op1=OP.add)
        vdrain()
        p.emit("vector", "tensor_tensor", ST[:, 44:48], ST[:, 28:32], ST[:, 40:44],
               op=OP.mult)
        vdrain()
        p.emit("vector", "tensor_tensor", ST[:, 16:20], ST[:, 8:12], ST[:, 44:48],
               op=OP.mult)
        vdrain()
        p.emit("vector", "tensor_scalar", ST[:, 48:52], ST[:, 16:20], -1.0, None,
               op0=OP.mult)
        p.emit("vector", "drain")
        for c in range(4):
            slot = c % 2
            xnt = fsl(XNT + slot * 2048, 2048)
            e = p.emit("vector", "tensor_scalar", xnt, fsl(XB + c * 2048, 2048),
                       ST[:, 44 + c:45 + c], ST[:, 48 + c:49 + c],
                       op0=OP.mult, op1=OP.add, waits=[xnt_free[slot]], inc="sE")
            xslot_free[c] = ("sE", e)
            for g in range(4):
                tb = g % 2
                for d4 in range(4):
                    w = []
                    if d4 == 0:
                        w = [("sE", e), tbank_free[tb]]
                    p.emit("tensor", "transpose",
                           bank(tb)[:, d4 * 128:(d4 + 1) * 128],
                           xnt[:, (4 * g + d4) * 128:(4 * g + d4 + 1) * 128],
                           IDN[:, :], waits=w, inc="sC" if d4 == 3 else None)
                cc = p.c("sC")
                dv = p.emit("scalar", "copy", xn_view(g, c),
                            bank(tb).rearrange("p (a b) -> p a b", a=4),
                            waits=[("sC", cc)], inc="sD")
                tbank_free[tb] = ("sD", dv)
                if g == 3:
                    xnt_free[slot] = ("sC", cc)

    for tile in range(8):
        ln_tile([d_xs[(tile * 4 + ci) * 128:(tile * 4 + ci + 1) * 128, :]
                 for ci in range(4)])
        xn_ready = p.c("sD")
        for fc in range(2):
            for dc in range(16):
                w = []
                if dc == 0:
                    w = [("sD", xn_ready), ("sB", const_sB), kbank_free[fc]]
                mm(bank(2 + fc), fsl(WKS + (fc * 16 + dc) * 128, 128),
                   fsl(XF + dc * 512, 512), dc == 0, dc == 15, waits=w,
                   inc="sC" if dc == 15 else None)
        kps = p.c("sC")
        for fc in range(2):
            p.emit("vector", "tensor_copy", fsl(KRW + fc * 512, 512), bank(2 + fc),
                   waits=[("sC", kps), krw_war] if fc == 0 else [], inc="sE")
        kraw_e = p.c("sE")
        kbank_free[0] = kbank_free[1] = ("sE", kraw_e)
        for fc in range(2):
            p.emit("scalar", "activation", fsl(KSQ + fc * 512, 512),
                   fsl(KRW + fc * 512, 512), AF.Square,
                   waits=[("sE", kraw_e)] if fc == 0 else [], inc="sD")
        ksq_d = p.c("sD")
        krw_war = ("sD", ksq_d)
        for fc in range(2):
            mm(bankrow(4), ONE[:, :], fsl(KSQ + fc * 512, 512), fc == 0, fc == 1,
               waits=[("sD", ksq_d), row4_free] if fc == 0 else [],
               inc="sC" if fc == 1 else None)
        ve = rsqrt_row(bankrow(4), 512, 1.0 / HD, RMS_EPS, [bc5_war], RWB1)
        row4_free = ("sE", ve)
        rrow_e = p.c("sE")
        bc = mm(bank(5), ONR[:, :], F[0:1, RWB1:RWB1 + 512], True, True,
                waits=[("sE", rrow_e), ("sE", const_sE)], inc="sC")
        bc5_war = ("sC", bc)
        ca = p.emit("gpsimd", "dma_start", out=fsl(CKS, 512),
                    in_=d_ck[:, tile * 512:(tile + 1) * 512],
                    waits=[cks_free], inc="sA", incn=16)
        ca = p.emit("gpsimd", "dma_start", out=fsl(SKS, 512),
                    in_=d_sk[:, tile * 512:(tile + 1) * 512], inc="sA", incn=16)
        for fc in range(2):
            p.emit("vector", "tensor_tensor", fsl(KSQ + fc * 512, 512),
                   fsl(KRW + fc * 512, 512), bank(5), op=OP.mult,
                   waits=[("sC", bc)] if fc == 0 else [])
            p.emit("vector", "tensor_scalar", fsl(KRW + fc * 512, 512),
                   fsl(KSQ + fc * 512, 512), GK[:, fc:fc + 1], None, op0=OP.mult)
        k0 = fsl(KRW, 512); k1 = fsl(KRW + 512, 512)
        t0 = fsl(KSQ, 512); t1 = fsl(KSQ + 512, 512)
        ck = fsl(CKS, 512); sk = fsl(SKS, 512)
        p.emit("vector", "tensor_tensor", t0, k0, ck, op=OP.mult, waits=[("sA", ca)])
        p.emit("vector", "tensor_tensor", t1, k1, sk, op=OP.mult)
        p.emit("vector", "tensor_tensor", fsl(KB + tile * 512, 512), t0, t1,
               op=OP.subtract)
        p.emit("vector", "tensor_tensor", t0, k0, sk, op=OP.mult)
        e = p.emit("vector", "tensor_tensor", t1, k1, ck, op=OP.mult, inc="sE")
        cks_free = ("sE", e)
        p.emit("vector", "tensor_tensor", fsl(KB + 4096 + tile * 512, 512), t0, t1,
               op=OP.add, inc="sE")
        for ci in range(4):
            vb = 6 + ci % 2
            for dc in range(16):
                w = []
                if dc == 0:
                    w = [("sD", xn_ready), ("sB", const_sB), vbank_free[ci % 2]]
                mm(bank(vb, 256), fsl(XF + dc * 512 + ci * 128, 128),
                   fsl(WVS + dc * 256, 256), dc == 0, dc == 15, waits=w,
                   inc="sC" if dc == 15 else None)
            dv = p.emit("scalar", "copy",
                        fsl(VB + (tile * 4 + ci) * 256, 256), bank(vb, 256),
                        waits=[("sC", p.c("sC"))], inc="sD")
            vbank_free[ci % 2] = ("sD", dv)

    for tile in range(2):
        ln_tile([d_xq[(tile * 4 + ci) * 128:(tile * 4 + ci + 1) * 128, :]
                 for ci in range(4)])
        rd = p.c("sD")
        for ci in range(4):
            tc = tile * 4 + ci
            v = fsl(XF, 8192).rearrange("p (dc t) -> p dc t", dc=16)
            p.emit("gpsimd", "dma_start", out=d_xnsp[tc],
                   in_=v[:, :, ci * 128:(ci + 1) * 128],
                   waits=[("sD", rd)] if ci == 0 else [], inc="sG", incn=16)
        p.emit("tensor", "nop", waits=[("sG", p.c("sG"))])

    p.barrier()

    if phases < 2:
        return _finish(nc, es, p, sems, dump, F, d_xnsp)
    p.emit("gpsimd", "dma_start", out=fsl(CQF, 1024), in_=d_cq[:, :], inc="sA", incn=16)
    p.emit("gpsimd", "dma_start", out=fsl(SQF, 1024), in_=d_sq[:, :], inc="sA", incn=16)
    cq_a = p.c("sA")
    wq_free = [("bar", 0), ("bar", 0)]
    qbank_free = [("sE", 0)] * 4
    row2_free = [("sE", 0), ("sE", 0)]
    bc3_war = [("sC", 0), ("sC", 0)]
    qrw_war = ("sD", 0)

    for half in range(2):
        xa = 0
        for k4 in range(4):
            v = fsl(XNH, 8192).rearrange("p (dc t) -> p dc t", dc=16)
            xa = p.emit("gpsimd", "dma_start", out=v[:, :, k4 * 128:(k4 + 1) * 128],
                        in_=d_xnsp[half * 4 + k4],
                        waits=[("sC", p.c("sC"))] if k4 == 0 else [],
                        inc="sA", incn=16)
        for h in range(H):
            hp2 = h % 2
            hb = 2 * hp2
            bslots = []
            for i in range(2):
                b = p.emit("sync", "dma_start",
                           out=fsl(WQS + i * 2048, 2048).rearrange(
                               "p (dc f) -> p dc f", dc=16),
                           in_=d_wq[2 * h + i], waits=[wq_free[i]],
                           inc="sB", incn=16)
                bslots.append(b)
            for i in range(2):
                for dc in range(16):
                    w = []
                    if dc == 0:
                        w = [("sB", bslots[i]), ("sA", xa), qbank_free[hb + i]]
                    mm(bank(hb + i), fsl(WQS + i * 2048 + dc * 128, 128),
                       fsl(XNH + dc * 512, 512), dc == 0, dc == 15, waits=w,
                       inc="sC" if dc == 15 else None)
                wq_free[i] = ("sC", p.c("sC"))
            qmm = p.c("sC")
            for i in range(2):
                p.emit("vector", "tensor_copy", fsl(QRW + i * 512, 512), bank(hb + i),
                       waits=[("sC", qmm), qrw_war] if i == 0 else [], inc="sE")
            qr = p.c("sE")
            qbank_free[hb] = qbank_free[hb + 1] = ("sE", qr)
            for i in range(2):
                p.emit("scalar", "activation", fsl(QS2 + i * 512, 512),
                       fsl(QRW + i * 512, 512), AF.Square,
                       waits=[("sE", qr)] if i == 0 else [], inc="sD")
            qsqd = p.c("sD")
            qrw_war = ("sD", qsqd)
            for i in range(2):
                mm(bankrow(4 + hp2), ONE[:, :], fsl(QS2 + i * 512, 512), i == 0, i == 1,
                   waits=[("sD", qsqd), row2_free[hp2]] if i == 0 else [],
                   inc="sC" if i == 1 else None)
            ve = rsqrt_row(bankrow(4 + hp2), 512, 1.0 / HD, RMS_EPS,
                           [bc3_war[0], bc3_war[1]], RWB2)
            row2_free[hp2] = ("sE", ve)
            bc = mm(bank(6 + hp2), ONR[:, :], F[0:1, RWB2:RWB2 + 512], True, True,
                    waits=[("sE", p.c("sE"))], inc="sC")
            bc3_war[hp2] = ("sC", bc)
            for i in range(2):
                fc = 2 * h + i
                p.emit("vector", "tensor_tensor", fsl(QS2 + i * 512, 512),
                       fsl(QRW + i * 512, 512), bank(6 + hp2), op=OP.mult,
                       waits=[("sC", bc), ("sA", cq_a)] if i == 0 else [])
                p.emit("vector", "tensor_scalar", fsl(QRW + i * 512, 512),
                       fsl(QS2 + i * 512, 512), GQ[:, fc:fc + 1], None, op0=OP.mult)
            q0 = fsl(QRW, 512); q1 = fsl(QRW + 512, 512)
            t0 = fsl(QS2, 512); t1 = fsl(QS2 + 512, 512)
            cq = fsl(CQF + half * 512, 512); sq = fsl(SQF + half * 512, 512)
            o0 = fsl(QB + h * 1024 + half * 512, 512)
            o1 = fsl(QB + 8192 + h * 1024 + half * 512, 512)
            p.emit("vector", "tensor_tensor", t0, q0, cq, op=OP.mult)
            p.emit("vector", "tensor_tensor", t1, q1, sq, op=OP.mult)
            p.emit("vector", "tensor_tensor", o0, t0, t1, op=OP.subtract)
            p.emit("vector", "tensor_tensor", t0, q0, sq, op=OP.mult)
            p.emit("vector", "tensor_tensor", t1, q1, cq, op=OP.mult)
            p.emit("vector", "tensor_tensor", o1, t0, t1, op=OP.add, inc="sE")

    p.barrier()

    if phases < 3:
        return _finish(nc, es, p, sems, dump, F, d_xnsp)
    sbank_free = [("bar", 0), ("bar", 0)]
    probs_free = [("bar", 0), ("bar", 0)]
    cbank_free = ("bar", 0)
    row4_free = ("sE", 0)
    wo_free = [("bar", 0), ("bar", 0)]
    otb_free = [("bar", 0), ("bar", 0)]
    b5war = ("bar", 0)
    b5_reader = ("sD", 0)
    otb_prev_qc = ("bar", 0)

    for qc in range(NQC):
        kext = KEXT[qc]
        ma = p.emit("gpsimd", "dma_start",
                    out=fsl(MSK, 2048).rearrange("p (k q) -> p k q", k=8),
                    in_=d_mask[qc], waits=[("sE", p.c("sE"))], inc="sA", incn=16)
        xq_a = 0
        for i in range(2):
            xq_a = p.emit("gpsimd", "dma_start", out=fsl(XQT + i * 2048, 2048),
                          in_=d_xq[qc * 256 + i * 128: qc * 256 + (i + 1) * 128, :],
                          waits=[("sC", p.c("sC"))] if i == 0 else [],
                          inc="sA", incn=16)
        for hp in range(H // 2):
            for kc in range(kext):
                sb = kc % 2
                for fc in range(2):
                    w = []
                    if fc == 0:
                        w = [sbank_free[sb]]
                        if kc == 0:
                            w += [cbank_free, row4_free]
                    rhs = fsl(QB + fc * 8192 + 2 * hp * 1024, 2048).rearrange(
                        "p (h t) -> p h t", h=2)[:, :, qc * 256:(qc + 1) * 256]
                    mm(bank(sb, 512), fsl(KB + fc * 4096 + kc * 128, 128),
                       rhs, fc == 0, fc == 1, waits=w,
                       inc="sC" if fc == 1 else None)
                sc = p.c("sC")
                prb = fsl(PRB + sb * 512, 512)
                d = p.emit("scalar", "activation", prb, bank(sb, 512), AF.Exp,
                           scale=1.0 / 16.0,
                           waits=[("sC", sc), probs_free[sb]], inc="sD")
                sbank_free[sb] = ("sD", d)
                pw = ("sD", d)
                if kc >= kext - NMASK:
                    for hh in range(2):
                        e = p.emit("vector", "tensor_tensor",
                                   fsl(PRB + sb * 512 + hh * 256, 256),
                                   fsl(PRB + sb * 512 + hh * 256, 256),
                                   fsl(MSK + (kc - (kext - NMASK)) * 256, 256),
                                   op=OP.mult,
                                   waits=[("sD", d), ("sA", ma)] if hh == 0 else [],
                                   inc="sE" if hh == 1 else None)
                    pw = ("sE", e)
                last = kc == kext - 1
                mm(bank(2, 512), fsl(VB + kc * 256, 128), prb, kc == 0, last,
                   waits=[pw])
                mm(bank(3, 512), fsl(VB + kc * 256 + 128, 128), prb, kc == 0, last)
                pe = mm(bankrow(4, 512), ONE[:, :], prb, kc == 0, last, inc="sC")
                probs_free[sb] = ("sC", pe)
            re = p.emit("vector", "reciprocal", F[0:1, ROW3:ROW3 + 512],
                        bankrow(4, 512),
                        waits=[("sC", p.c("sC")), bc5_war], inc="sE")
            row4_free = ("sE", re)
            bc = mm(bank(5, 512), ONR[:, :], F[0:1, ROW3:ROW3 + 512], True, True,
                    waits=[("sE", re), b5war], inc="sC")
            bc5_war = ("sC", bc)
            d = p.emit("scalar", "copy", fsl(RBS, 512), bank(5, 512),
                       waits=[("sC", bc), cbank_free], inc="sD")
            b5_reader = ("sD", d)
            for i in range(2):
                for hh in range(2):
                    fc = 4 * hp + 2 * hh + i
                    p.emit("vector", "tensor_tensor",
                           fsl(CTX + fc * 256, 256),
                           bank(2 + i, 512)[:, hh * 256:(hh + 1) * 256],
                           fsl(RBS + hh * 256, 256), op=OP.mult,
                           waits=[("sD", d)] if (i == 0 and hh == 0) else [],
                           inc="sE" if (i == 1 and hh == 1) else None)
            cbank_free = ("sE", p.c("sE"))
        ctx_e = p.c("sE")
        first_mm_done = False
        for fc in range(16):
            slot = fc % 2
            b = p.emit("sync", "dma_start",
                       out=fsl(WOS + slot * 2048, 2048),
                       in_=d_wo[fc], waits=[wo_free[slot]], inc="sB", incn=16)
            for qsub in range(2):
                for og in range(4):
                    bk = qsub * 4 + og
                    w = []
                    if qsub == 0 and og == 0:
                        w = [("sB", b)]
                        if fc == 0:
                            w += [("sE", ctx_e), cbank_free, b5_reader,
                                  sbank_free[0], sbank_free[1], row4_free,
                                  otb_prev_qc]
                    mm(bank(bk, 512), fsl(CTX + fc * 256 + qsub * 128, 128),
                       fsl(WOS + slot * 2048 + og * 512, 512),
                       fc == 0, fc == 15, waits=w,
                       inc="sC" if (qsub == 1 and og == 3) else None)
            wo_free[slot] = ("sC", p.c("sC"))
        omm = p.c("sC")
        for qsub in range(2):
            for og in range(4):
                slot = (qsub * 4 + og) % 2
                w = [otb_free[slot]]
                if qsub == 0 and og == 0:
                    w += [("sC", omm), ("sA", xq_a)]
                e = p.emit("vector", "tensor_tensor", fsl(OTB + slot * 512, 512),
                           bank(qsub * 4 + og, 512),
                           fsl(XQT + qsub * 2048 + og * 512, 512), op=OP.add,
                           waits=w, inc="sE")
                g = p.emit("gpsimd", "dma_start",
                           out=d_out[qc * 256 + qsub * 128: qc * 256 + (qsub + 1) * 128,
                                     og * 512:(og + 1) * 512],
                           in_=fsl(OTB + slot * 512, 512), waits=[("sE", e)],
                           inc="sG", incn=16)
                otb_free[slot] = ("sG", g)
        allfree = ("sE", p.c("sE"))
        otb_prev_qc = allfree
        sbank_free = [allfree, allfree]
        cbank_free = allfree
        row4_free = allfree
        b5war = allfree

    p.barrier()

    if phases < 4:
        return _finish(nc, es, p, sems, dump, F, d_xnsp)
    wgu_free = [("bar", 0), ("bar", 0)]
    wd_free = [("bar", 0), ("bar", 0)]
    gu_free = [("bar", 0)] * 8
    rbt_free = [("bar", 0), ("bar", 0)]
    otf_free = [("bar", 0), ("bar", 0)]
    tbank_free4 = [("bar", 0), ("bar", 0)]
    store_before_p4 = p.c("sG")
    RBT2 = SCR
    OTF2 = SCR + 1536

    for qt in range(4):
        xa = 0
        v = fsl(XNQ, 4096).rearrange("p (dc t) -> p dc t", dc=16)
        for k2 in range(2):
            w = ([("sE", p.c("sE")), ("sC", p.c("sC")), ("sG", p.c("sG"))]
                 if k2 == 0 else [])
            xa = p.emit("gpsimd", "dma_start", out=v[:, :, k2 * 128:(k2 + 1) * 128],
                        in_=d_xnsp[qt * 2 + k2], waits=w, inc="sA", incn=16)
        hmark = [0, 0]
        for k in range(8):
            for dc in range(16):
                slot = dc % 2
                b = p.emit("sync", "dma_start",
                           out=fsl(WGU + slot * 2048, 2048).rearrange(
                               "p (gu c) -> p gu c", gu=2),
                           in_=d_wgu[dc, k], waits=[wgu_free[slot]],
                           inc="sB", incn=16)
                for tch in range(2):
                    for m in range(4):
                        bk = tch * 4 + m
                        w = []
                        if m == 0:
                            w = [("sB", b)]
                            if dc == 0:
                                w += [("sA", xa), gu_free[bk],
                                      gu_free[tch * 4 + 1], gu_free[tch * 4 + 2],
                                      gu_free[tch * 4 + 3]]
                        mm(bank(bk, 512), fsl(XNQ + dc * 256 + tch * 128, 128),
                           fsl(WGU + slot * 2048 + (m // 2) * 1024 + (m % 2) * 512, 512),
                           dc == 0, dc == 15, waits=w,
                           inc="sC" if (tch == 1 and m == 3) else None)
                wgu_free[slot] = ("sC", p.c("sC"))
            gmm = p.c("sC")
            for tch in range(2):
                scr = fsl(SCR + tch * 1024, 1024)
                d = p.emit("scalar", "activation", scr,
                           PS[:, (tch * 4) * 512:(tch * 4) * 512 + 1024], AF.Silu,
                           waits=[("sC", gmm)] + ([("sE", p.c("sE"))] if k > 0 else []),
                           inc="sD")
                e = p.emit("vector", "tensor_tensor",
                           fsl(HTM + tch * 8192 + k * 1024, 1024), scr,
                           PS[:, (tch * 4 + 2) * 512:(tch * 4 + 2) * 512 + 1024],
                           op=OP.mult, waits=[("sD", d)], inc="sE")
                gu_free[tch * 4 + 0] = ("sD", d)
                gu_free[tch * 4 + 1] = ("sD", d)
                gu_free[tch * 4 + 2] = ("sE", e)
                gu_free[tch * 4 + 3] = ("sE", e)
                hmark[tch] = e
        hall = p.c("sE")
        spmark = [0, 0]
        for tch in range(2):
            base = HTM + tch * 8192
            for gh in range(2):
                J = [fsl(base + j * 2048 + gh * 1024, 1024) for j in range(4)]
                A, Bb, C, Dd, E = [fsl(SCR + i * 1024, 1024) for i in range(5)]
                w0 = [("sE", hall)] if (tch == 0 and gh == 0) else [("sE", p.c("sE"))]
                p.emit("scalar", "activation", A, J[0], AF.Abs, waits=w0)
                p.emit("scalar", "activation", Bb, J[1], AF.Abs)
                p.emit("scalar", "activation", C, J[2], AF.Abs)
                p.emit("scalar", "activation", Dd, J[3], AF.Abs, inc="sD")
                wd0 = [("sD", p.c("sD"))]
                p.emit("vector", "tensor_tensor", E, A, Bb, op=OP.max, waits=wd0)
                p.emit("vector", "tensor_tensor", A, A, Bb, op=OP.min)
                p.emit("vector", "tensor_tensor", Bb, C, Dd, op=OP.max)
                p.emit("vector", "tensor_tensor", C, C, Dd, op=OP.min)
                p.emit("vector", "tensor_tensor", E, E, Bb, op=OP.min)
                p.emit("vector", "tensor_tensor", A, A, C, op=OP.max)
                p.emit("vector", "tensor_tensor", E, E, A, op=OP.max, inc="sE")
                p.emit("scalar", "activation", A, J[0], AF.Abs,
                       waits=[("sE", p.c("sE"))])
                p.emit("scalar", "activation", Bb, J[1], AF.Abs)
                p.emit("scalar", "activation", C, J[2], AF.Abs, inc="sD")
                wd1 = [("sD", p.c("sD"))]
                for j, src in enumerate((A, Bb, C, Dd)):
                    p.emit("vector", "tensor_tensor", src, src, E, op=OP.is_ge,
                           waits=wd1 if j == 0 else [])
                    p.emit("vector", "tensor_tensor", J[j], J[j], src, op=OP.mult,
                           inc="sE" if j == 3 else None)
            spmark[tch] = p.c("sE")
        hfm_view = fsl(HFM, 16384).rearrange("p (ic t) -> p ic t", ic=64)
        for tch in range(2):
            base = HTM + tch * 8192
            for g4 in range(16):
                tb = g4 % 2
                for i4 in range(4):
                    w = []
                    if i4 == 0:
                        w = [tbank_free4[tb]]
                        if g4 < 2:
                            w.append(("sE", spmark[tch]))
                    p.emit("tensor", "transpose",
                           bank(tb)[:, i4 * 128:(i4 + 1) * 128],
                           fsl(base + (g4 * 4 + i4) * 128, 128), IDN[:, :],
                           waits=w, inc="sC" if i4 == 3 else None)
                d = p.emit("scalar", "copy",
                           hfm_view[:, g4 * 4:g4 * 4 + 4, tch * 128:(tch + 1) * 128],
                           bank(tb).rearrange("p (a b) -> p a b", a=4),
                           waits=[("sC", p.c("sC"))], inc="sD")
                tbank_free4[tb] = ("sD", d)
        hfm_done = p.c("sD")
        for ic in range(64):
            slot = ic % 2
            b = 0
            for hh in range(2):
                b = p.emit("sync", "dma_start",
                           out=fsl(WDS + (2 * slot + hh) * 1024, 1024),
                           in_=d_wd[ic][:, hh * 1024:(hh + 1) * 1024],
                           waits=[wd_free[slot]] if hh == 0 else [],
                           inc="sB", incn=16)
            for tch in range(2):
                for dg in range(4):
                    bk = tch * 4 + dg
                    w = []
                    if dg == 0:
                        w = [("sB", b)]
                        if ic == 0:
                            w += [("sD", hfm_done), tbank_free4[0], tbank_free4[1],
                                  gu_free[bk]]
                    mm(bank(bk, 512), fsl(HFM + ic * 256 + tch * 128, 128),
                       fsl(WDS + (2 * slot + dg // 2) * 1024 + (dg % 2) * 512, 512),
                       ic == 0, ic == 63, waits=w,
                       inc="sC" if (tch == 1 and dg == 3) else None)
            wd_free[slot] = ("sC", p.c("sC"))
        dmm = p.c("sC")
        for tch in range(2):
            for dg in range(4):
                slot = (tch * 4 + dg) % 2
                g = p.emit("gpsimd", "dma_start", out=fsl(RBT2 + slot * 512, 512),
                           in_=d_out[qt * 256 + tch * 128: qt * 256 + (tch + 1) * 128,
                                     dg * 512:(dg + 1) * 512],
                           waits=[("sG", store_before_p4), ("sC", dmm),
                                  rbt_free[slot]],
                           inc="sR", incn=16)
                e = p.emit("vector", "tensor_tensor", fsl(OTF2 + slot * 512, 512),
                           bank(tch * 4 + dg, 512), fsl(RBT2 + slot * 512, 512),
                           op=OP.add,
                           waits=[("sC", dmm), ("sR", g), otf_free[slot]], inc="sE")
                gu_free[tch * 4 + dg] = ("sE", e)
                rbt_free[slot] = ("sE", e)
                g2 = p.emit("gpsimd", "dma_start",
                            out=d_out[qt * 256 + tch * 128: qt * 256 + (tch + 1) * 128,
                                      dg * 512:(dg + 1) * 512],
                            in_=fsl(OTF2 + slot * 512, 512), waits=[("sE", e)],
                            inc="sG", incn=16)
                otf_free[slot] = ("sG", g2)

    p.barrier()
    return _finish(nc, es, p, sems, dump, F, d_xnsp)


def _finish(nc, es, p, sems, dump=False, F=None, d_xnsp=None):
    if dump:
        d_dbg = nc.dram_tensor("dbgF", [128, POOL], F32, kind="ExternalOutput").ap()
        d_dbg2 = nc.dram_tensor("dbgsp", [8, 128, D], F32, kind="ExternalOutput").ap()
        dbgb = es.enter_context(nc.sbuf_tensor("dbgb", [128, 256], F32)).ap()
        nch = (POOL + 4095) // 4096
        for i in range(nch):
            n = min(4096, POOL - i * 4096)
            p.emit("gpsimd", "dma_start", out=d_dbg[:, i*4096:i*4096+n],
                   in_=F[:, i*4096:i*4096+n], inc="sG", incn=16)
        for tc in range(8):
            for i in range(8):
                p.emit("gpsimd", "dma_start", out=dbgb[:, :],
                       in_=d_xnsp[tc][:, i*256:(i+1)*256],
                       waits=[("sG", p.c("sG"))], inc="sA", incn=16)
                p.emit("gpsimd", "dma_start", out=d_dbg2[tc][:, i*256:(i+1)*256],
                       in_=dbgb[:, :], waits=[("sA", p.c("sA"))], inc="sG", incn=16)
        p.emit("gpsimd", "nop", waits=[("sG", p.c("sG"))])
    with es, nc.Block() as block:
        def mk(engname):
            def body(eng):
                for op, args, kwargs, ws, inc, incn in p.ops[engname]:
                    for (sn, val) in ws:
                        eng.wait_ge(sems[sn], val)
                    inst = getattr(eng, op)(*args, **kwargs)
                    if inc is not None:
                        inst.then_inc(sems[inc], incn)
            return body
        block.tensor(mk("tensor"))
        block.vector(mk("vector"))
        block.scalar(mk("scalar"))
        block.gpsimd(mk("gpsimd"))
        block.sync(mk("sync"))
    return nc



def _core_inputs(c, x, shared, ckf_h, skf_h):
    seq, j = c // 4, c % 4
    chunks = [j, 7 - j, 8 + j, 15 - j]
    own_idx = np.concatenate([np.arange(ch * 256, (ch + 1) * 256) for ch in chunks])
    xs = np.ascontiguousarray(x[seq])
    xq = np.ascontiguousarray(x[seq][own_idx])
    cqf = np.ascontiguousarray(ckf_h[:, own_idx])
    sqf = np.ascontiguousarray(skf_h[:, own_idx])
    mask = np.zeros((NQC, 128, NMASK, 256), np.float32)
    for qc in range(NQC):
        base = KEXT[qc] * 128 - 1024
        kpos = base + np.arange(1024)
        qpos = own_idx[qc * 256:(qc + 1) * 256]
        m = (kpos[:, None] <= qpos[None, :]).astype(np.float32)
        mask[qc] = m.reshape(NMASK, 128, 256).transpose(1, 0, 2)
    d = {"xs": xs, "xq": xq, "ckf": ckf_h, "skf": skf_h,
         "cqf": cqf, "sqf": sqf, "mask": mask}
    d.update(shared)
    return d, own_idx


def prep_inputs(inputs):
    x = np.asarray(inputs["L0_attn_x"], np.float32)
    q_proj = np.asarray(inputs["L0_attn_q_proj"], np.float32)
    k_proj = np.asarray(inputs["L0_attn_k_proj"], np.float32)
    v_proj = np.asarray(inputs["L0_attn_v_proj"], np.float32)
    o_proj = np.asarray(inputs["L0_attn_o_proj"], np.float32)
    q_norm = np.asarray(inputs["L0_attn_q_norm"], np.float32)
    k_norm = np.asarray(inputs["L0_attn_k_norm"], np.float32)
    w_gate = np.asarray(inputs["L0_ffn_w_gate"], np.float32)
    w_up = np.asarray(inputs["L0_ffn_w_up"], np.float32)
    w_down = np.asarray(inputs["L0_ffn_w_down"], np.float32)

    wq_h = np.ascontiguousarray(q_proj.reshape(16, 128, 16, 128).transpose(0, 3, 2, 1))
    wk_h = np.ascontiguousarray(k_proj.reshape(2, 128, 16, 128).transpose(3, 0, 2, 1))
    wv_h = np.ascontiguousarray(v_proj.reshape(256, 16, 128).transpose(2, 1, 0))
    wo_h = np.ascontiguousarray(o_proj.T.reshape(16, 128, 2048))
    g2 = w_gate.reshape(2048, 2048, 4).transpose(0, 2, 1).reshape(2048, 8192)
    u2 = w_up.reshape(2048, 2048, 4).transpose(0, 2, 1).reshape(2048, 8192)
    Gr = g2.reshape(16, 128, 8, 1024)
    Ur = u2.reshape(16, 128, 8, 1024)
    wgu_h = np.ascontiguousarray(np.stack([Gr, Ur], axis=3).transpose(0, 2, 1, 3, 4))
    wd_h = np.ascontiguousarray(
        w_down.reshape(2048, 4, 2048).transpose(1, 0, 2).reshape(64, 128, 2048))
    gq_h = np.ascontiguousarray(np.tile(q_norm, 8).reshape(16, 128).T)
    gk_h = np.ascontiguousarray(k_norm.reshape(2, 128).T)

    idx = np.arange(128, dtype=np.float64)
    inv = 1.0 / (1e6 ** (2.0 * idx / 256.0))
    pos = np.arange(L, dtype=np.float64)
    fr = np.outer(inv, pos)
    ckf_h = np.cos(fr).astype(np.float32)
    skf_h = np.sin(fr).astype(np.float32)

    shared = {"wq": wq_h, "wk": wk_h, "wv": wv_h, "wo": wo_h, "wgu": wgu_h,
              "wd": wd_h, "gq": gq_h, "gk": gk_h,
              "ident": np.eye(128, dtype=np.float32)}

    in_maps, own_idxs = [], []
    for c in range(NCORE):
        m, oi = _core_inputs(c, x, shared, ckf_h, skf_h)
        in_maps.append(m)
        own_idxs.append(oi)
    return in_maps, own_idxs


def kernel(**inputs):
    in_maps, own_idxs = prep_inputs(inputs)
    nc = build_program()
    res = run_bass_kernel_spmd(nc, in_maps, core_ids=list(range(NCORE)))
    out = np.empty((B, L, D), np.float32)
    for c in range(NCORE):
        out[c // 4, own_idxs[c]] = np.asarray(res.results[c]["out"])
    return out


# version 10
# speedup vs baseline: 1.0098x; 1.0098x over previous
import contextlib
import numpy as np

import concourse.bass as bass
import concourse.mybir as mybir
from concourse.bass_utils import run_bass_kernel_spmd
from concourse.alu_op_type import AluOpType as OP

F32 = mybir.dt.float32
AF = mybir.ActivationFunctionType
AX = mybir.AxisListType

ENGINES = ("tensor", "vector", "scalar", "gpsimd", "sync")

B, L, D = 2, 4096, 2048
H, HD = 8, 256
NCORE = 8
OWN = 1024
NQC = 4
KEXT = [8, 16, 24, 32]
NMASK = 8
LN_EPS = 1e-5
RMS_EPS = 1e-6

POOL = 51200
KB = 0
VB = 8192
QB = 16384
WKS = 16384
WVS = 20480
XF = 24576
XB = 32768
XSQ = 36864
XNT = 38912
KRW = 43008
KSQ = 44032
CKS = 45056
SKS = 45568
RWB1 = 46080
XNH = 32768
WQS = 40960
QRW = 45056
QS2 = 46080
CQF = 47104
SQF = 48128
RWB2 = 49152
CTX = 32768
PRB = 36864
MSK = 37888
XQT = 39936
RBS = 44032
OTB = 44544
WOS = 45568
ROW3 = 49664
HFM = 0
HTM = 16384
XNQ = 32768
WGU = 36864
WDS = 40960
SCR = 45056
RBT = 50176
OTF = 50688


class Prog:
    def __init__(self):
        self.ops = {e: [] for e in ENGINES}
        self.cnt = {}

    def emit(self, eng, opname, *args, waits=(), inc=None, incn=1, **kwargs):
        ws = [(s, v) for (s, v) in waits if v > 0]
        self.ops[eng].append((opname, args, kwargs, ws, inc, incn))
        if inc is not None:
            self.cnt[inc] = self.cnt.get(inc, 0) + incn
        return self.cnt.get(inc, 0)

    def c(self, sem):
        return self.cnt.get(sem, 0)

    def barrier(self):
        self.emit("gpsimd", "nop",
                  waits=[("sA", self.c("sA")), ("sG", self.c("sG")),
                         ("sR", self.c("sR"))], inc="bar")
        self.emit("sync", "nop", waits=[("sB", self.c("sB"))], inc="bar")
        self.emit("vector", "drain")
        self.emit("scalar", "drain")
        for e in ("tensor", "vector", "scalar"):
            self.emit(e, "nop", inc="bar")
        tgt = self.c("bar")
        for e in ENGINES:
            self.emit(e, "nop", waits=[("bar", tgt)])


def build_program(phases=4, dump=False):
    nc = bass.Bass("TRN2", target_bir_lowering=False, debug=False,
                   detect_race_conditions=False)

    d_xs = nc.dram_tensor("xs", [L, D], F32, kind="ExternalInput").ap()
    d_xq = nc.dram_tensor("xq", [OWN, D], F32, kind="ExternalInput").ap()
    d_ck = nc.dram_tensor("ckf", [128, L], F32, kind="ExternalInput").ap()
    d_sk = nc.dram_tensor("skf", [128, L], F32, kind="ExternalInput").ap()
    d_cq = nc.dram_tensor("cqf", [128, OWN], F32, kind="ExternalInput").ap()
    d_sq = nc.dram_tensor("sqf", [128, OWN], F32, kind="ExternalInput").ap()
    d_mask = nc.dram_tensor("mask", [NQC, 128, NMASK, 256], F32, kind="ExternalInput").ap()
    d_wq = nc.dram_tensor("wq", [16, 128, 16, 128], F32, kind="ExternalInput").ap()
    d_wk = nc.dram_tensor("wk", [128, 2, 16, 128], F32, kind="ExternalInput").ap()
    d_wv = nc.dram_tensor("wv", [128, 16, 256], F32, kind="ExternalInput").ap()
    d_wo = nc.dram_tensor("wo", [16, 128, 2048], F32, kind="ExternalInput").ap()
    d_wgu = nc.dram_tensor("wgu", [16, 8, 128, 2, 1024], F32, kind="ExternalInput").ap()
    d_wd = nc.dram_tensor("wd", [64, 128, 2048], F32, kind="ExternalInput").ap()
    d_gq = nc.dram_tensor("gq", [128, 16], F32, kind="ExternalInput").ap()
    d_gk = nc.dram_tensor("gk", [128, 2], F32, kind="ExternalInput").ap()
    d_id = nc.dram_tensor("ident", [128, 128], F32, kind="ExternalInput").ap()
    d_out = nc.dram_tensor("out", [OWN, D], F32, kind="ExternalOutput").ap()
    d_xnsp = nc.dram_tensor("xnsp", [8, 128, D], F32)

    es = contextlib.ExitStack()
    F = es.enter_context(nc.sbuf_tensor("F", [128, POOL], F32)).ap()
    IDN = es.enter_context(nc.sbuf_tensor("IDN", [128, 128], F32)).ap()
    ONE = es.enter_context(nc.sbuf_tensor("ONE", [128, 1], F32)).ap()
    ONR = es.enter_context(nc.sbuf_tensor("ONR", [1, 128], F32)).ap()
    GQ = es.enter_context(nc.sbuf_tensor("GQ", [128, 16], F32)).ap()
    GK = es.enter_context(nc.sbuf_tensor("GK", [128, 2], F32)).ap()
    ST = es.enter_context(nc.sbuf_tensor("ST", [128, 16], F32)).ap()
    PS = es.enter_context(nc.psum_tensor("PS", [128, 4096], F32)).ap()
    sems = {}
    for s in ("sA", "sB", "sC", "sD", "sE", "sG", "sR", "bar"):
        sems[s] = es.enter_context(nc.semaphore(s))

    p = Prog()

    def bank(b, n=512):
        return PS[:, b * 512:b * 512 + n]

    def bankrow(b, n=512):
        return PS[0:1, b * 512:b * 512 + n]

    def fsl(off, n):
        return F[:, off:off + n]

    def mm(out_ap, lhsT, rhs, start, stop, waits=(), inc=None):
        return p.emit("tensor", "matmul", out_ap, lhsT, rhs, start=start,
                      stop=stop, waits=waits, inc=inc)

    def vdrain():
        p.emit("vector", "drain")

    def rsqrt_cols(vc, wv):
        p.emit("scalar", "activation", ST[:, 6:7], ST[:, vc:vc + 1], AF.Sqrt,
               waits=wv, inc="sD")
        wd = [("sD", p.c("sD"))]
        p.emit("vector", "reciprocal", ST[:, 7:8], ST[:, 6:7], waits=wd)
        vdrain()
        p.emit("vector", "tensor_tensor", ST[:, 8:9], ST[:, 7:8], ST[:, 7:8], op=OP.mult)
        vdrain()
        p.emit("vector", "tensor_tensor", ST[:, 9:10], ST[:, 8:9], ST[:, vc:vc + 1], op=OP.mult)
        vdrain()
        p.emit("vector", "tensor_scalar", ST[:, 10:11], ST[:, 9:10], -0.5, 1.5,
               op0=OP.mult, op1=OP.add)
        vdrain()
        p.emit("vector", "tensor_tensor", ST[:, 11:12], ST[:, 7:8], ST[:, 10:11], op=OP.mult)

    def rsqrt_row(src_row, n, scale, eps, wout, base):
        veps = F[0:1, base + 512:base + 512 + n]
        Ar = F[0:1, base + 1024:base + 1024 + n]
        Br = F[0:1, base + 1536:base + 1536 + n]
        out = F[0:1, base:base + n]
        e = p.emit("vector", "tensor_scalar", veps, src_row, scale, eps,
                   op0=OP.mult, op1=OP.add, waits=[("sC", p.c("sC"))], inc="sE")
        p.emit("scalar", "activation", Ar, veps, AF.Sqrt,
               waits=[("sE", e)], inc="sD")
        p.emit("vector", "reciprocal", Br, Ar, waits=[("sD", p.c("sD"))])
        vdrain()
        p.emit("vector", "tensor_tensor", Ar, Br, Br, op=OP.mult)
        vdrain()
        p.emit("vector", "tensor_tensor", out, Ar, veps, op=OP.mult, waits=wout)
        vdrain()
        p.emit("vector", "tensor_scalar", Ar, out, -0.5, 1.5, op0=OP.mult, op1=OP.add)
        vdrain()
        p.emit("vector", "tensor_tensor", out, Br, Ar, op=OP.mult, inc="sE")
        return e

    p.emit("gpsimd", "dma_start", out=IDN[:, :], in_=d_id[:, :], inc="sA", incn=16)
    p.emit("gpsimd", "dma_start", out=GQ[:, :], in_=d_gq[:, :], inc="sA", incn=16)
    p.emit("gpsimd", "dma_start", out=GK[:, :], in_=d_gk[:, :], inc="sA", incn=16)
    p.emit("sync", "dma_start",
           out=fsl(WKS, 4096).rearrange("p (a b) -> p a b", a=32),
           in_=d_wk.rearrange("p fc dc f -> p (fc dc) f"), inc="sB", incn=16)
    p.emit("sync", "dma_start",
           out=fsl(WVS, 4096).rearrange("p (a b) -> p a b", a=16),
           in_=d_wv[:, :, :], inc="sB", incn=16)
    p.emit("vector", "memset", ONE[:, :], 1.0)
    p.emit("vector", "memset", ONR[:, :], 1.0, inc="sE")
    const_sE = p.c("sE")
    const_sA = p.c("sA")
    const_sB = p.c("sB")

    xslot_free = [("sE", 0), ("sE", 0)]
    xnt_free = [("sC", 0), ("sC", 0)]
    tbank_free = [("sD", 0), ("sD", 0)]
    kbank_free = [("sE", 0), ("sE", 0)]
    vbank_free = [("sD", 0), ("sD", 0)]
    cks_free = ("sE", 0)
    row4_free = ("sE", 0)
    krw_war = ("sD", 0)
    bc5_war = ("sC", 0)

    def xn_view(dcg, ci):
        v = fsl(XF, 8192).rearrange("p (dc t) -> p dc t", dc=16)
        return v[:, 4 * dcg:4 * dcg + 4, ci * 128:(ci + 1) * 128]

    def ln_chunk(src_ap, slot, ci):
        xb = fsl(XB + slot * 2048, 2048)
        xnt = fsl(XNT + slot * 2048, 2048)
        a = p.emit("gpsimd", "dma_start", out=xb, in_=src_ap,
                   waits=[xslot_free[slot], ("sA", const_sA)], inc="sA", incn=16)
        wA = [("sA", a)]
        p.emit("vector", "tensor_reduce", ST[:, 0:1], xb, axis=AX.X, op=OP.add,
               waits=wA)
        vdrain()
        p.emit("scalar", "activation", fsl(XSQ, 2048), xb, AF.Square,
               accum_out=ST[:, 1:2], waits=wA, inc="sD")
        wsq = [("sD", p.c("sD"))]
        p.emit("vector", "tensor_scalar", ST[:, 2:3], ST[:, 0:1], 1.0 / D, None,
               op0=OP.mult)
        p.emit("vector", "tensor_scalar", ST[:, 3:4], ST[:, 1:2], 1.0 / D, LN_EPS,
               op0=OP.mult, op1=OP.add, waits=wsq)
        vdrain()
        p.emit("vector", "tensor_tensor", ST[:, 4:5], ST[:, 2:3], ST[:, 2:3], op=OP.mult)
        vdrain()
        p.emit("vector", "tensor_tensor", ST[:, 5:6], ST[:, 3:4], ST[:, 4:5],
               op=OP.subtract, inc="sE")
        rsqrt_cols(5, [("sE", p.c("sE"))])
        vdrain()
        p.emit("vector", "tensor_tensor", ST[:, 12:13], ST[:, 2:3], ST[:, 11:12], op=OP.mult)
        vdrain()
        p.emit("vector", "tensor_scalar", ST[:, 13:14], ST[:, 12:13], -1.0, None,
               op0=OP.mult)
        p.emit("vector", "drain")
        e = p.emit("vector", "tensor_scalar", xnt, xb, ST[:, 11:12], ST[:, 13:14],
                   op0=OP.mult, op1=OP.add, waits=[xnt_free[slot]], inc="sE")
        xslot_free[slot] = ("sE", e)
        for g in range(4):
            tb = g % 2
            for d4 in range(4):
                w = []
                if d4 == 0:
                    w = [("sE", e), tbank_free[tb]]
                p.emit("tensor", "transpose", bank(tb)[:, d4 * 128:(d4 + 1) * 128],
                       xnt[:, (4 * g + d4) * 128:(4 * g + d4 + 1) * 128], IDN[:, :],
                       waits=w, inc="sC" if d4 == 3 else None)
            c = p.c("sC")
            dv = p.emit("scalar", "copy", xn_view(g, ci),
                        bank(tb).rearrange("p (a b) -> p a b", a=4),
                        waits=[("sC", c)], inc="sD")
            tbank_free[tb] = ("sD", dv)
            if g == 3:
                xnt_free[slot] = ("sC", c)

    for tile in range(8):
        for ci in range(4):
            ln_chunk(d_xs[(tile * 4 + ci) * 128:(tile * 4 + ci + 1) * 128, :],
                     (tile * 4 + ci) % 2, ci)
        xn_ready = p.c("sD")
        for fc in range(2):
            for dc in range(16):
                w = []
                if dc == 0:
                    w = [("sD", xn_ready), ("sB", const_sB), kbank_free[fc]]
                mm(bank(2 + fc), fsl(WKS + (fc * 16 + dc) * 128, 128),
                   fsl(XF + dc * 512, 512), dc == 0, dc == 15, waits=w,
                   inc="sC" if dc == 15 else None)
        kps = p.c("sC")
        for fc in range(2):
            p.emit("vector", "tensor_copy", fsl(KRW + fc * 512, 512), bank(2 + fc),
                   waits=[("sC", kps), krw_war] if fc == 0 else [], inc="sE")
        kraw_e = p.c("sE")
        kbank_free[0] = kbank_free[1] = ("sE", kraw_e)
        for fc in range(2):
            p.emit("scalar", "activation", fsl(KSQ + fc * 512, 512),
                   fsl(KRW + fc * 512, 512), AF.Square,
                   waits=[("sE", kraw_e)] if fc == 0 else [], inc="sD")
        ksq_d = p.c("sD")
        krw_war = ("sD", ksq_d)
        for fc in range(2):
            mm(bankrow(4), ONE[:, :], fsl(KSQ + fc * 512, 512), fc == 0, fc == 1,
               waits=[("sD", ksq_d), row4_free] if fc == 0 else [],
               inc="sC" if fc == 1 else None)
        ve = rsqrt_row(bankrow(4), 512, 1.0 / HD, RMS_EPS, [bc5_war], RWB1)
        row4_free = ("sE", ve)
        rrow_e = p.c("sE")
        bc = mm(bank(5), ONR[:, :], F[0:1, RWB1:RWB1 + 512], True, True,
                waits=[("sE", rrow_e), ("sE", const_sE)], inc="sC")
        bc5_war = ("sC", bc)
        ca = p.emit("gpsimd", "dma_start", out=fsl(CKS, 512),
                    in_=d_ck[:, tile * 512:(tile + 1) * 512],
                    waits=[cks_free], inc="sA", incn=16)
        ca = p.emit("gpsimd", "dma_start", out=fsl(SKS, 512),
                    in_=d_sk[:, tile * 512:(tile + 1) * 512], inc="sA", incn=16)
        for fc in range(2):
            p.emit("vector", "tensor_tensor", fsl(KSQ + fc * 512, 512),
                   fsl(KRW + fc * 512, 512), bank(5), op=OP.mult,
                   waits=[("sC", bc)] if fc == 0 else [])
            p.emit("vector", "tensor_scalar", fsl(KRW + fc * 512, 512),
                   fsl(KSQ + fc * 512, 512), GK[:, fc:fc + 1], None, op0=OP.mult)
        k0 = fsl(KRW, 512); k1 = fsl(KRW + 512, 512)
        t0 = fsl(KSQ, 512); t1 = fsl(KSQ + 512, 512)
        ck = fsl(CKS, 512); sk = fsl(SKS, 512)
        p.emit("vector", "tensor_tensor", t0, k0, ck, op=OP.mult, waits=[("sA", ca)])
        p.emit("vector", "tensor_tensor", t1, k1, sk, op=OP.mult)
        p.emit("vector", "tensor_tensor", fsl(KB + tile * 512, 512), t0, t1,
               op=OP.subtract)
        p.emit("vector", "tensor_tensor", t0, k0, sk, op=OP.mult)
        e = p.emit("vector", "tensor_tensor", t1, k1, ck, op=OP.mult, inc="sE")
        cks_free = ("sE", e)
        p.emit("vector", "tensor_tensor", fsl(KB + 4096 + tile * 512, 512), t0, t1,
               op=OP.add, inc="sE")
        for ci in range(4):
            vb = 6 + ci % 2
            for dc in range(16):
                w = []
                if dc == 0:
                    w = [("sD", xn_ready), ("sB", const_sB), vbank_free[ci % 2]]
                mm(bank(vb, 256), fsl(XF + dc * 512 + ci * 128, 128),
                   fsl(WVS + dc * 256, 256), dc == 0, dc == 15, waits=w,
                   inc="sC" if dc == 15 else None)
            dv = p.emit("scalar", "copy",
                        fsl(VB + (tile * 4 + ci) * 256, 256), bank(vb, 256),
                        waits=[("sC", p.c("sC"))], inc="sD")
            vbank_free[ci % 2] = ("sD", dv)

    for tile in range(2):
        for ci in range(4):
            tc = tile * 4 + ci
            ln_chunk(d_xq[tc * 128:(tc + 1) * 128, :], tc % 2, ci)
        rd = p.c("sD")
        for ci in range(4):
            tc = tile * 4 + ci
            v = fsl(XF, 8192).rearrange("p (dc t) -> p dc t", dc=16)
            p.emit("gpsimd", "dma_start", out=d_xnsp[tc],
                   in_=v[:, :, ci * 128:(ci + 1) * 128],
                   waits=[("sD", rd)] if ci == 0 else [], inc="sG", incn=16)
        p.emit("tensor", "nop", waits=[("sG", p.c("sG"))])

    p.barrier()

    if phases < 2:
        return _finish(nc, es, p, sems, dump, F, d_xnsp)
    p.emit("gpsimd", "dma_start", out=fsl(CQF, 1024), in_=d_cq[:, :], inc="sA", incn=16)
    p.emit("gpsimd", "dma_start", out=fsl(SQF, 1024), in_=d_sq[:, :], inc="sA", incn=16)
    cq_a = p.c("sA")
    wq_free = [("bar", 0), ("bar", 0)]
    qbank_free = [("sE", 0), ("sE", 0)]
    row2_free = ("sE", 0)
    bc3_war = ("sC", 0)
    qrw_war = ("sD", 0)

    for half in range(2):
        xa = 0
        for k4 in range(4):
            v = fsl(XNH, 8192).rearrange("p (dc t) -> p dc t", dc=16)
            xa = p.emit("gpsimd", "dma_start", out=v[:, :, k4 * 128:(k4 + 1) * 128],
                        in_=d_xnsp[half * 4 + k4],
                        waits=[("sC", p.c("sC"))] if k4 == 0 else [],
                        inc="sA", incn=16)
        for h in range(H):
            bslots = []
            for i in range(2):
                b = p.emit("sync", "dma_start",
                           out=fsl(WQS + i * 2048, 2048).rearrange(
                               "p (dc f) -> p dc f", dc=16),
                           in_=d_wq[2 * h + i], waits=[wq_free[i]],
                           inc="sB", incn=16)
                bslots.append(b)
            for i in range(2):
                for dc in range(16):
                    w = []
                    if dc == 0:
                        w = [("sB", bslots[i]), ("sA", xa), qbank_free[i]]
                    mm(bank(i), fsl(WQS + i * 2048 + dc * 128, 128),
                       fsl(XNH + dc * 512, 512), dc == 0, dc == 15, waits=w,
                       inc="sC" if dc == 15 else None)
                wq_free[i] = ("sC", p.c("sC"))
            qmm = p.c("sC")
            for i in range(2):
                p.emit("vector", "tensor_copy", fsl(QRW + i * 512, 512), bank(i),
                       waits=[("sC", qmm), qrw_war] if i == 0 else [], inc="sE")
            qr = p.c("sE")
            qbank_free[0] = qbank_free[1] = ("sE", qr)
            for i in range(2):
                p.emit("scalar", "activation", fsl(QS2 + i * 512, 512),
                       fsl(QRW + i * 512, 512), AF.Square,
                       waits=[("sE", qr)] if i == 0 else [], inc="sD")
            qsqd = p.c("sD")
            qrw_war = ("sD", qsqd)
            for i in range(2):
                mm(bankrow(2), ONE[:, :], fsl(QS2 + i * 512, 512), i == 0, i == 1,
                   waits=[("sD", qsqd), row2_free] if i == 0 else [],
                   inc="sC" if i == 1 else None)
            ve = rsqrt_row(bankrow(2), 512, 1.0 / HD, RMS_EPS, [bc3_war], RWB2)
            row2_free = ("sE", ve)
            bc = mm(bank(3), ONR[:, :], F[0:1, RWB2:RWB2 + 512], True, True,
                    waits=[("sE", p.c("sE"))], inc="sC")
            bc3_war = ("sC", bc)
            for i in range(2):
                fc = 2 * h + i
                p.emit("vector", "tensor_tensor", fsl(QS2 + i * 512, 512),
                       fsl(QRW + i * 512, 512), bank(3), op=OP.mult,
                       waits=[("sC", bc), ("sA", cq_a)] if i == 0 else [])
                p.emit("vector", "tensor_scalar", fsl(QRW + i * 512, 512),
                       fsl(QS2 + i * 512, 512), GQ[:, fc:fc + 1], None, op0=OP.mult)
            q0 = fsl(QRW, 512); q1 = fsl(QRW + 512, 512)
            t0 = fsl(QS2, 512); t1 = fsl(QS2 + 512, 512)
            cq = fsl(CQF + half * 512, 512); sq = fsl(SQF + half * 512, 512)
            o0 = fsl(QB + h * 1024 + half * 512, 512)
            o1 = fsl(QB + 8192 + h * 1024 + half * 512, 512)
            p.emit("vector", "tensor_tensor", t0, q0, cq, op=OP.mult)
            p.emit("vector", "tensor_tensor", t1, q1, sq, op=OP.mult)
            p.emit("vector", "tensor_tensor", o0, t0, t1, op=OP.subtract)
            p.emit("vector", "tensor_tensor", t0, q0, sq, op=OP.mult)
            p.emit("vector", "tensor_tensor", t1, q1, cq, op=OP.mult)
            p.emit("vector", "tensor_tensor", o1, t0, t1, op=OP.add, inc="sE")

    p.barrier()

    if phases < 3:
        return _finish(nc, es, p, sems, dump, F, d_xnsp)
    sbank_free = [("bar", 0), ("bar", 0)]
    probs_free = [("bar", 0), ("bar", 0)]
    cbank_free = ("bar", 0)
    row4_free = ("sE", 0)
    wo_free = [("bar", 0), ("bar", 0)]
    otb_free = [("bar", 0), ("bar", 0)]
    b5war = ("bar", 0)
    b5_reader = ("sD", 0)
    otb_prev_qc = ("bar", 0)

    for qc in range(NQC):
        kext = KEXT[qc]
        ma = p.emit("gpsimd", "dma_start",
                    out=fsl(MSK, 2048).rearrange("p (k q) -> p k q", k=8),
                    in_=d_mask[qc], waits=[("sE", p.c("sE"))], inc="sA", incn=16)
        xq_a = 0
        for i in range(2):
            xq_a = p.emit("gpsimd", "dma_start", out=fsl(XQT + i * 2048, 2048),
                          in_=d_xq[qc * 256 + i * 128: qc * 256 + (i + 1) * 128, :],
                          waits=[("sC", p.c("sC"))] if i == 0 else [],
                          inc="sA", incn=16)
        for hp in range(H // 2):
            for kc in range(kext):
                sb = kc % 2
                for fc in range(2):
                    w = []
                    if fc == 0:
                        w = [sbank_free[sb]]
                        if kc == 0:
                            w += [cbank_free, row4_free]
                    rhs = fsl(QB + fc * 8192 + 2 * hp * 1024, 2048).rearrange(
                        "p (h t) -> p h t", h=2)[:, :, qc * 256:(qc + 1) * 256]
                    mm(bank(sb, 512), fsl(KB + fc * 4096 + kc * 128, 128),
                       rhs, fc == 0, fc == 1, waits=w,
                       inc="sC" if fc == 1 else None)
                sc = p.c("sC")
                prb = fsl(PRB + sb * 512, 512)
                d = p.emit("scalar", "activation", prb, bank(sb, 512), AF.Exp,
                           scale=1.0 / 16.0,
                           waits=[("sC", sc), probs_free[sb]], inc="sD")
                sbank_free[sb] = ("sD", d)
                pw = ("sD", d)
                if kc >= kext - NMASK:
                    for hh in range(2):
                        e = p.emit("vector", "tensor_tensor",
                                   fsl(PRB + sb * 512 + hh * 256, 256),
                                   fsl(PRB + sb * 512 + hh * 256, 256),
                                   fsl(MSK + (kc - (kext - NMASK)) * 256, 256),
                                   op=OP.mult,
                                   waits=[("sD", d), ("sA", ma)] if hh == 0 else [],
                                   inc="sE" if hh == 1 else None)
                    pw = ("sE", e)
                last = kc == kext - 1
                mm(bank(2, 512), fsl(VB + kc * 256, 128), prb, kc == 0, last,
                   waits=[pw])
                mm(bank(3, 512), fsl(VB + kc * 256 + 128, 128), prb, kc == 0, last)
                pe = mm(bankrow(4, 512), ONE[:, :], prb, kc == 0, last, inc="sC")
                probs_free[sb] = ("sC", pe)
            re = p.emit("vector", "reciprocal", F[0:1, ROW3:ROW3 + 512],
                        bankrow(4, 512),
                        waits=[("sC", p.c("sC")), bc5_war], inc="sE")
            row4_free = ("sE", re)
            bc = mm(bank(5, 512), ONR[:, :], F[0:1, ROW3:ROW3 + 512], True, True,
                    waits=[("sE", re), b5war], inc="sC")
            bc5_war = ("sC", bc)
            d = p.emit("scalar", "copy", fsl(RBS, 512), bank(5, 512),
                       waits=[("sC", bc), cbank_free], inc="sD")
            b5_reader = ("sD", d)
            for i in range(2):
                for hh in range(2):
                    fc = 4 * hp + 2 * hh + i
                    p.emit("vector", "tensor_tensor",
                           fsl(CTX + fc * 256, 256),
                           bank(2 + i, 512)[:, hh * 256:(hh + 1) * 256],
                           fsl(RBS + hh * 256, 256), op=OP.mult,
                           waits=[("sD", d)] if (i == 0 and hh == 0) else [],
                           inc="sE" if (i == 1 and hh == 1) else None)
            cbank_free = ("sE", p.c("sE"))
        ctx_e = p.c("sE")
        first_mm_done = False
        for fc in range(16):
            slot = fc % 2
            b = p.emit("sync", "dma_start",
                       out=fsl(WOS + slot * 2048, 2048),
                       in_=d_wo[fc], waits=[wo_free[slot]], inc="sB", incn=16)
            for qsub in range(2):
                for og in range(4):
                    bk = qsub * 4 + og
                    w = []
                    if qsub == 0 and og == 0:
                        w = [("sB", b)]
                        if fc == 0:
                            w += [("sE", ctx_e), cbank_free, b5_reader,
                                  sbank_free[0], sbank_free[1], row4_free,
                                  otb_prev_qc]
                    mm(bank(bk, 512), fsl(CTX + fc * 256 + qsub * 128, 128),
                       fsl(WOS + slot * 2048 + og * 512, 512),
                       fc == 0, fc == 15, waits=w,
                       inc="sC" if (qsub == 1 and og == 3) else None)
            wo_free[slot] = ("sC", p.c("sC"))
        omm = p.c("sC")
        for qsub in range(2):
            for og in range(4):
                slot = (qsub * 4 + og) % 2
                w = [otb_free[slot]]
                if qsub == 0 and og == 0:
                    w += [("sC", omm), ("sA", xq_a)]
                e = p.emit("vector", "tensor_tensor", fsl(OTB + slot * 512, 512),
                           bank(qsub * 4 + og, 512),
                           fsl(XQT + qsub * 2048 + og * 512, 512), op=OP.add,
                           waits=w, inc="sE")
                g = p.emit("gpsimd", "dma_start",
                           out=d_out[qc * 256 + qsub * 128: qc * 256 + (qsub + 1) * 128,
                                     og * 512:(og + 1) * 512],
                           in_=fsl(OTB + slot * 512, 512), waits=[("sE", e)],
                           inc="sG", incn=16)
                otb_free[slot] = ("sG", g)
        allfree = ("sE", p.c("sE"))
        otb_prev_qc = allfree
        sbank_free = [allfree, allfree]
        cbank_free = allfree
        row4_free = allfree
        b5war = allfree

    p.barrier()

    if phases < 4:
        return _finish(nc, es, p, sems, dump, F, d_xnsp)
    wgu_free = [("bar", 0), ("bar", 0)]
    wd_free = [("bar", 0), ("bar", 0)]
    gu_free = [("bar", 0)] * 8
    rbt_free = [("bar", 0), ("bar", 0)]
    otf_free = [("bar", 0), ("bar", 0)]
    tbank_free4 = [("bar", 0), ("bar", 0)]
    store_before_p4 = p.c("sG")
    RBT2 = SCR
    OTF2 = SCR + 1536

    for qt in range(4):
        xa = 0
        v = fsl(XNQ, 4096).rearrange("p (dc t) -> p dc t", dc=16)
        for k2 in range(2):
            w = ([("sE", p.c("sE")), ("sC", p.c("sC")), ("sG", p.c("sG"))]
                 if k2 == 0 else [])
            xa = p.emit("gpsimd", "dma_start", out=v[:, :, k2 * 128:(k2 + 1) * 128],
                        in_=d_xnsp[qt * 2 + k2], waits=w, inc="sA", incn=16)
        hmark = [0, 0]
        for k in range(8):
            for dc in range(16):
                slot = dc % 2
                b = p.emit("sync", "dma_start",
                           out=fsl(WGU + slot * 2048, 2048).rearrange(
                               "p (gu c) -> p gu c", gu=2),
                           in_=d_wgu[dc, k], waits=[wgu_free[slot]],
                           inc="sB", incn=16)
                for tch in range(2):
                    for m in range(4):
                        bk = tch * 4 + m
                        w = []
                        if m == 0:
                            w = [("sB", b)]
                            if dc == 0:
                                w += [("sA", xa), gu_free[bk],
                                      gu_free[tch * 4 + 1], gu_free[tch * 4 + 2],
                                      gu_free[tch * 4 + 3]]
                        mm(bank(bk, 512), fsl(XNQ + dc * 256 + tch * 128, 128),
                           fsl(WGU + slot * 2048 + (m // 2) * 1024 + (m % 2) * 512, 512),
                           dc == 0, dc == 15, waits=w,
                           inc="sC" if (tch == 1 and m == 3) else None)
                wgu_free[slot] = ("sC", p.c("sC"))
            gmm = p.c("sC")
            for tch in range(2):
                scr = fsl(SCR + tch * 1024, 1024)
                d = p.emit("scalar", "activation", scr,
                           PS[:, (tch * 4) * 512:(tch * 4) * 512 + 1024], AF.Silu,
                           waits=[("sC", gmm)] + ([("sE", p.c("sE"))] if k > 0 else []),
                           inc="sD")
                e = p.emit("vector", "tensor_tensor",
                           fsl(HTM + tch * 8192 + k * 1024, 1024), scr,
                           PS[:, (tch * 4 + 2) * 512:(tch * 4 + 2) * 512 + 1024],
                           op=OP.mult, waits=[("sD", d)], inc="sE")
                gu_free[tch * 4 + 0] = ("sD", d)
                gu_free[tch * 4 + 1] = ("sD", d)
                gu_free[tch * 4 + 2] = ("sE", e)
                gu_free[tch * 4 + 3] = ("sE", e)
                hmark[tch] = e
        hall = p.c("sE")
        spmark = [[0, 0], [0, 0]]
        for tch in range(2):
            base = HTM + tch * 8192
            for gh in range(2):
                J = [fsl(base + j * 2048 + gh * 1024, 1024) for j in range(4)]
                A, Bb, C, Dd, E = [fsl(SCR + i * 1024, 1024) for i in range(5)]
                w0 = [("sE", hall)] if (tch == 0 and gh == 0) else [("sE", p.c("sE"))]
                p.emit("scalar", "activation", A, J[0], AF.Abs, waits=w0)
                p.emit("scalar", "activation", Bb, J[1], AF.Abs)
                p.emit("scalar", "activation", C, J[2], AF.Abs)
                p.emit("scalar", "activation", Dd, J[3], AF.Abs, inc="sD")
                wd0 = [("sD", p.c("sD"))]
                p.emit("vector", "tensor_tensor", E, A, Bb, op=OP.max, waits=wd0)
                p.emit("vector", "tensor_tensor", A, A, Bb, op=OP.min)
                p.emit("vector", "tensor_tensor", Bb, C, Dd, op=OP.max)
                p.emit("vector", "tensor_tensor", C, C, Dd, op=OP.min)
                p.emit("vector", "tensor_tensor", E, E, Bb, op=OP.min)
                p.emit("vector", "tensor_tensor", A, A, C, op=OP.max)
                p.emit("vector", "tensor_tensor", E, E, A, op=OP.max, inc="sE")
                p.emit("scalar", "activation", A, J[0], AF.Abs,
                       waits=[("sE", p.c("sE"))])
                p.emit("scalar", "activation", Bb, J[1], AF.Abs)
                p.emit("scalar", "activation", C, J[2], AF.Abs, inc="sD")
                wd1 = [("sD", p.c("sD"))]
                for j, src in enumerate((A, Bb, C, Dd)):
                    p.emit("vector", "tensor_tensor", src, src, E, op=OP.is_ge,
                           waits=wd1 if j == 0 else [])
                    p.emit("vector", "tensor_tensor", J[j], J[j], src, op=OP.mult,
                           inc="sE" if j == 3 else None)
                spmark[tch][gh] = p.c("sE")
        hfm_view = fsl(HFM, 16384).rearrange("p (ic t) -> p ic t", ic=64)
        g4order = [g for g in range(16) if (g % 4) < 2] + \
                  [g for g in range(16) if (g % 4) >= 2]
        for tch in range(2):
            base = HTM + tch * 8192
            for gi, g4 in enumerate(g4order):
                gh = (g4 % 4) // 2
                tb = g4 % 2
                for i4 in range(4):
                    w = []
                    if i4 == 0:
                        w = [tbank_free4[tb]]
                        if gi in (0, 8):
                            w.append(("sE", spmark[tch][gh]))
                    p.emit("tensor", "transpose",
                           bank(tb)[:, i4 * 128:(i4 + 1) * 128],
                           fsl(base + (g4 * 4 + i4) * 128, 128), IDN[:, :],
                           waits=w, inc="sC" if i4 == 3 else None)
                d = p.emit("scalar", "copy",
                           hfm_view[:, g4 * 4:g4 * 4 + 4, tch * 128:(tch + 1) * 128],
                           bank(tb).rearrange("p (a b) -> p a b", a=4),
                           waits=[("sC", p.c("sC"))], inc="sD")
                tbank_free4[tb] = ("sD", d)
        hfm_done = p.c("sD")
        for ic in range(64):
            slot = ic % 2
            b = 0
            for hh in range(2):
                b = p.emit("sync", "dma_start",
                           out=fsl(WDS + (2 * slot + hh) * 1024, 1024),
                           in_=d_wd[ic][:, hh * 1024:(hh + 1) * 1024],
                           waits=[wd_free[slot]] if hh == 0 else [],
                           inc="sB", incn=16)
            for tch in range(2):
                for dg in range(4):
                    bk = tch * 4 + dg
                    w = []
                    if dg == 0:
                        w = [("sB", b)]
                        if ic == 0:
                            w += [("sD", hfm_done), tbank_free4[0], tbank_free4[1],
                                  gu_free[bk]]
                    mm(bank(bk, 512), fsl(HFM + ic * 256 + tch * 128, 128),
                       fsl(WDS + (2 * slot + dg // 2) * 1024 + (dg % 2) * 512, 512),
                       ic == 0, ic == 63, waits=w,
                       inc="sC" if (tch == 1 and dg == 3) else None)
            wd_free[slot] = ("sC", p.c("sC"))
        dmm = p.c("sC")
        for tch in range(2):
            for dg in range(4):
                slot = (tch * 4 + dg) % 2
                g = p.emit("gpsimd", "dma_start", out=fsl(RBT2 + slot * 512, 512),
                           in_=d_out[qt * 256 + tch * 128: qt * 256 + (tch + 1) * 128,
                                     dg * 512:(dg + 1) * 512],
                           waits=[("sG", store_before_p4), ("sC", dmm),
                                  rbt_free[slot]],
                           inc="sR", incn=16)
                e = p.emit("vector", "tensor_tensor", fsl(OTF2 + slot * 512, 512),
                           bank(tch * 4 + dg, 512), fsl(RBT2 + slot * 512, 512),
                           op=OP.add,
                           waits=[("sC", dmm), ("sR", g), otf_free[slot]], inc="sE")
                gu_free[tch * 4 + dg] = ("sE", e)
                rbt_free[slot] = ("sE", e)
                g2 = p.emit("gpsimd", "dma_start",
                            out=d_out[qt * 256 + tch * 128: qt * 256 + (tch + 1) * 128,
                                      dg * 512:(dg + 1) * 512],
                            in_=fsl(OTF2 + slot * 512, 512), waits=[("sE", e)],
                            inc="sG", incn=16)
                otf_free[slot] = ("sG", g2)

    p.barrier()
    return _finish(nc, es, p, sems, dump, F, d_xnsp)


def _finish(nc, es, p, sems, dump=False, F=None, d_xnsp=None):
    if dump:
        d_dbg = nc.dram_tensor("dbgF", [128, POOL], F32, kind="ExternalOutput").ap()
        d_dbg2 = nc.dram_tensor("dbgsp", [8, 128, D], F32, kind="ExternalOutput").ap()
        dbgb = es.enter_context(nc.sbuf_tensor("dbgb", [128, 256], F32)).ap()
        nch = (POOL + 4095) // 4096
        for i in range(nch):
            n = min(4096, POOL - i * 4096)
            p.emit("gpsimd", "dma_start", out=d_dbg[:, i*4096:i*4096+n],
                   in_=F[:, i*4096:i*4096+n], inc="sG", incn=16)
        for tc in range(8):
            for i in range(8):
                p.emit("gpsimd", "dma_start", out=dbgb[:, :],
                       in_=d_xnsp[tc][:, i*256:(i+1)*256],
                       waits=[("sG", p.c("sG"))], inc="sA", incn=16)
                p.emit("gpsimd", "dma_start", out=d_dbg2[tc][:, i*256:(i+1)*256],
                       in_=dbgb[:, :], waits=[("sA", p.c("sA"))], inc="sG", incn=16)
        p.emit("gpsimd", "nop", waits=[("sG", p.c("sG"))])
    with es, nc.Block() as block:
        def mk(engname):
            def body(eng):
                for op, args, kwargs, ws, inc, incn in p.ops[engname]:
                    for (sn, val) in ws:
                        eng.wait_ge(sems[sn], val)
                    inst = getattr(eng, op)(*args, **kwargs)
                    if inc is not None:
                        inst.then_inc(sems[inc], incn)
            return body
        block.tensor(mk("tensor"))
        block.vector(mk("vector"))
        block.scalar(mk("scalar"))
        block.gpsimd(mk("gpsimd"))
        block.sync(mk("sync"))
    return nc



def _core_inputs(c, x, shared, ckf_h, skf_h):
    seq, j = c // 4, c % 4
    chunks = [j, 7 - j, 8 + j, 15 - j]
    own_idx = np.concatenate([np.arange(ch * 256, (ch + 1) * 256) for ch in chunks])
    xs = np.ascontiguousarray(x[seq])
    xq = np.ascontiguousarray(x[seq][own_idx])
    cqf = np.ascontiguousarray(ckf_h[:, own_idx])
    sqf = np.ascontiguousarray(skf_h[:, own_idx])
    mask = np.zeros((NQC, 128, NMASK, 256), np.float32)
    for qc in range(NQC):
        base = KEXT[qc] * 128 - 1024
        kpos = base + np.arange(1024)
        qpos = own_idx[qc * 256:(qc + 1) * 256]
        m = (kpos[:, None] <= qpos[None, :]).astype(np.float32)
        mask[qc] = m.reshape(NMASK, 128, 256).transpose(1, 0, 2)
    d = {"xs": xs, "xq": xq, "ckf": ckf_h, "skf": skf_h,
         "cqf": cqf, "sqf": sqf, "mask": mask}
    d.update(shared)
    return d, own_idx


def prep_inputs(inputs):
    x = np.asarray(inputs["L0_attn_x"], np.float32)
    q_proj = np.asarray(inputs["L0_attn_q_proj"], np.float32)
    k_proj = np.asarray(inputs["L0_attn_k_proj"], np.float32)
    v_proj = np.asarray(inputs["L0_attn_v_proj"], np.float32)
    o_proj = np.asarray(inputs["L0_attn_o_proj"], np.float32)
    q_norm = np.asarray(inputs["L0_attn_q_norm"], np.float32)
    k_norm = np.asarray(inputs["L0_attn_k_norm"], np.float32)
    w_gate = np.asarray(inputs["L0_ffn_w_gate"], np.float32)
    w_up = np.asarray(inputs["L0_ffn_w_up"], np.float32)
    w_down = np.asarray(inputs["L0_ffn_w_down"], np.float32)

    wq_h = np.ascontiguousarray(q_proj.reshape(16, 128, 16, 128).transpose(0, 3, 2, 1))
    wk_h = np.ascontiguousarray(k_proj.reshape(2, 128, 16, 128).transpose(3, 0, 2, 1))
    wv_h = np.ascontiguousarray(v_proj.reshape(256, 16, 128).transpose(2, 1, 0))
    wo_h = np.ascontiguousarray(o_proj.T.reshape(16, 128, 2048))
    g2 = w_gate.reshape(2048, 2048, 4).transpose(0, 2, 1).reshape(2048, 8192)
    u2 = w_up.reshape(2048, 2048, 4).transpose(0, 2, 1).reshape(2048, 8192)
    Gr = g2.reshape(16, 128, 8, 1024)
    Ur = u2.reshape(16, 128, 8, 1024)
    wgu_h = np.ascontiguousarray(np.stack([Gr, Ur], axis=3).transpose(0, 2, 1, 3, 4))
    wd_h = np.ascontiguousarray(
        w_down.reshape(2048, 4, 2048).transpose(1, 0, 2).reshape(64, 128, 2048))
    gq_h = np.ascontiguousarray(np.tile(q_norm, 8).reshape(16, 128).T)
    gk_h = np.ascontiguousarray(k_norm.reshape(2, 128).T)

    idx = np.arange(128, dtype=np.float64)
    inv = 1.0 / (1e6 ** (2.0 * idx / 256.0))
    pos = np.arange(L, dtype=np.float64)
    fr = np.outer(inv, pos)
    ckf_h = np.cos(fr).astype(np.float32)
    skf_h = np.sin(fr).astype(np.float32)

    shared = {"wq": wq_h, "wk": wk_h, "wv": wv_h, "wo": wo_h, "wgu": wgu_h,
              "wd": wd_h, "gq": gq_h, "gk": gk_h,
              "ident": np.eye(128, dtype=np.float32)}

    in_maps, own_idxs = [], []
    for c in range(NCORE):
        m, oi = _core_inputs(c, x, shared, ckf_h, skf_h)
        in_maps.append(m)
        own_idxs.append(oi)
    return in_maps, own_idxs


def kernel(**inputs):
    in_maps, own_idxs = prep_inputs(inputs)
    nc = build_program()
    res = run_bass_kernel_spmd(nc, in_maps, core_ids=list(range(NCORE)))
    out = np.empty((B, L, D), np.float32)
    for c in range(NCORE):
        out[c // 4, own_idxs[c]] = np.asarray(res.results[c]["out"])
    return out


# version 13
# speedup vs baseline: 1.2851x; 1.2726x over previous
import contextlib
import numpy as np

import concourse.bass as bass
import concourse.mybir as mybir
from concourse.bass_utils import run_bass_kernel_spmd
from concourse.alu_op_type import AluOpType as OP

F32 = mybir.dt.float32
BF16 = mybir.dt.bfloat16
AF = mybir.ActivationFunctionType
AX = mybir.AxisListType

ENGINES = ("tensor", "vector", "scalar", "gpsimd", "sync")

B, L, D = 2, 4096, 2048
H, HD = 8, 256
NCORE = 8
OWN = 1024
NQC = 4
KEXT = [8, 16, 24, 32]
NMASK = 8
LN_EPS = 1e-5
RMS_EPS = 1e-6

POOL = 51200
KB = 0
VB = 8192
QB = 16384
WKS = 16384
WVS = 20480
XF = 24576
XB = 32768
XSQ = 36864
XNT = 38912
KRW = 43008
KSQ = 44032
CKS = 45056
SKS = 45568
RWB1 = 46080
XNH = 32768
WQS = 40960
QRW = 45056
QS2 = 46080
CQF = 47104
SQF = 48128
RWB2 = 49152
CTX = 32768
PRB = 36864
MSK = 37888
XQT = 39936
RBS = 44032
OTB = 44544
WOS = 45568
ROW3 = 49664
XNH_E = 0
XNL_E = 16384
WR_E = 32768
NWR = 17
XNF_U = 16384
WF_U = 33792
HQ_U = 36864
SP_U = 45056
HQH_E = 90112
HQL_E = 94208
HRD_E = 0
WDB_E = 32768
WDF_U = 33792
RBT_U = 38912
OTF_U = 39936


class Prog:
    def __init__(self):
        self.ops = {e: [] for e in ENGINES}
        self.cnt = {}

    def emit(self, eng, opname, *args, waits=(), inc=None, incn=1, **kwargs):
        ws = [(s, v) for (s, v) in waits if v > 0]
        self.ops[eng].append((opname, args, kwargs, ws, inc, incn))
        if inc is not None:
            self.cnt[inc] = self.cnt.get(inc, 0) + incn
        return self.cnt.get(inc, 0)

    def c(self, sem):
        return self.cnt.get(sem, 0)

    def barrier(self):
        self.emit("gpsimd", "nop",
                  waits=[("sA", self.c("sA")), ("sG", self.c("sG")),
                         ("sR", self.c("sR"))], inc="bar")
        self.emit("sync", "nop", waits=[("sB", self.c("sB"))], inc="bar")
        self.emit("vector", "drain")
        self.emit("scalar", "drain")
        for e in ("tensor", "vector", "scalar"):
            self.emit(e, "nop", inc="bar")
        tgt = self.c("bar")
        for e in ENGINES:
            self.emit(e, "nop", waits=[("bar", tgt)])


def build_program(phases=4, dump=False):
    nc = bass.Bass("TRN2", target_bir_lowering=False, debug=False,
                   detect_race_conditions=False)

    d_xs = nc.dram_tensor("xs", [L, D], F32, kind="ExternalInput").ap()
    d_xq = nc.dram_tensor("xq", [OWN, D], F32, kind="ExternalInput").ap()
    d_ck = nc.dram_tensor("ckf", [128, L], F32, kind="ExternalInput").ap()
    d_sk = nc.dram_tensor("skf", [128, L], F32, kind="ExternalInput").ap()
    d_cq = nc.dram_tensor("cqf", [128, OWN], F32, kind="ExternalInput").ap()
    d_sq = nc.dram_tensor("sqf", [128, OWN], F32, kind="ExternalInput").ap()
    d_mask = nc.dram_tensor("mask", [NQC, 128, NMASK, 256], F32, kind="ExternalInput").ap()
    d_wq = nc.dram_tensor("wq", [16, 128, 16, 128], F32, kind="ExternalInput").ap()
    d_wk = nc.dram_tensor("wk", [128, 2, 16, 128], F32, kind="ExternalInput").ap()
    d_wv = nc.dram_tensor("wv", [128, 16, 256], F32, kind="ExternalInput").ap()
    d_wo = nc.dram_tensor("wo", [16, 128, 2048], F32, kind="ExternalInput").ap()
    d_wgu = nc.dram_tensor("wgu", [16, 16, 128, 1024], F32, kind="ExternalInput").ap()
    d_wd = nc.dram_tensor("wd", [64, 128, 2048], F32, kind="ExternalInput").ap()
    if dump:
        d_hsp = nc.dram_tensor("hsp", [16, 128, 2, 4096], BF16,
                               kind="ExternalOutput")
    else:
        d_hsp = nc.dram_tensor("hsp", [16, 128, 2, 4096], BF16)
    d_gq = nc.dram_tensor("gq", [128, 16], F32, kind="ExternalInput").ap()
    d_gk = nc.dram_tensor("gk", [128, 2], F32, kind="ExternalInput").ap()
    d_id = nc.dram_tensor("ident", [128, 128], F32, kind="ExternalInput").ap()
    d_out = nc.dram_tensor("out", [OWN, D], F32, kind="ExternalOutput").ap()
    d_xnsp = nc.dram_tensor("xnsp", [8, 128, D], F32)

    es = contextlib.ExitStack()
    fbase = (nc.sbuf_base + 31) // 32 * 32
    F = es.enter_context(nc.sbuf_tensor("F", [128, POOL], F32)).ap()

    def alias(name, n_elem, unit_off, dt):
        return nc.alloc_sbuf_tensor_at(
            name, [128, n_elem], dt, offset=fbase + 4 * unit_off).ap()

    XNFa = alias("XNFa", 16384, XNF_U, F32)
    WFa = alias("WFa", 3072, WF_U, F32)
    HQa = alias("HQa", 8192, HQ_U, F32)
    SPa = alias("SPa", 5120, SP_U, F32)
    XNHa = alias("XNHa", 16384, 0, BF16)
    XNLa = alias("XNLa", 16384, 8192, BF16)
    WRa = alias("WRa", NWR * 2048, 16384, BF16)
    HQHa = alias("HQHa", 4096, SP_U, BF16)
    HQLa = alias("HQLa", 4096, SP_U + 2048, BF16)
    HRDa = alias("HRDa", 65536, 0, BF16)
    WDBa = alias("WDBa", 16384, 40960, BF16)
    WDFa = alias("WDFa", 4096, WDF_U, F32)
    RBTa = alias("RBTa", 1024, 49152, F32)
    OTFa = alias("OTFa", 1024, 50176, F32)
    IDN = es.enter_context(nc.sbuf_tensor("IDN", [128, 128], F32)).ap()
    ONE = es.enter_context(nc.sbuf_tensor("ONE", [128, 1], F32)).ap()
    ONR = es.enter_context(nc.sbuf_tensor("ONR", [1, 128], F32)).ap()
    GQ = es.enter_context(nc.sbuf_tensor("GQ", [128, 16], F32)).ap()
    GK = es.enter_context(nc.sbuf_tensor("GK", [128, 2], F32)).ap()
    ST = es.enter_context(nc.sbuf_tensor("ST", [128, 16], F32)).ap()
    PS = es.enter_context(nc.psum_tensor("PS", [128, 4096], F32)).ap()
    sems = {}
    for s in ("sA", "sB", "sC", "sD", "sE", "sG", "sR", "bar"):
        sems[s] = es.enter_context(nc.semaphore(s))

    p = Prog()

    def bank(b, n=512):
        return PS[:, b * 512:b * 512 + n]

    def bankrow(b, n=512):
        return PS[0:1, b * 512:b * 512 + n]

    def fsl(off, n):
        return F[:, off:off + n]

    def fsb(off_e, n):
        return Fb[:, off_e:off_e + n]

    def mm(out_ap, lhsT, rhs, start, stop, waits=(), inc=None):
        return p.emit("tensor", "matmul", out_ap, lhsT, rhs, start=start,
                      stop=stop, waits=waits, inc=inc)

    def vdrain():
        p.emit("vector", "drain")

    def rsqrt_cols(vc, wv):
        p.emit("scalar", "activation", ST[:, 6:7], ST[:, vc:vc + 1], AF.Sqrt,
               waits=wv, inc="sD")
        wd = [("sD", p.c("sD"))]
        p.emit("vector", "reciprocal", ST[:, 7:8], ST[:, 6:7], waits=wd)
        vdrain()
        p.emit("vector", "tensor_tensor", ST[:, 8:9], ST[:, 7:8], ST[:, 7:8], op=OP.mult)
        vdrain()
        p.emit("vector", "tensor_tensor", ST[:, 9:10], ST[:, 8:9], ST[:, vc:vc + 1], op=OP.mult)
        vdrain()
        p.emit("vector", "tensor_scalar", ST[:, 10:11], ST[:, 9:10], -0.5, 1.5,
               op0=OP.mult, op1=OP.add)
        vdrain()
        p.emit("vector", "tensor_tensor", ST[:, 11:12], ST[:, 7:8], ST[:, 10:11], op=OP.mult)

    def rsqrt_row(src_row, n, scale, eps, wout, base):
        veps = F[0:1, base + 512:base + 512 + n]
        Ar = F[0:1, base + 1024:base + 1024 + n]
        Br = F[0:1, base + 1536:base + 1536 + n]
        out = F[0:1, base:base + n]
        e = p.emit("vector", "tensor_scalar", veps, src_row, scale, eps,
                   op0=OP.mult, op1=OP.add, waits=[("sC", p.c("sC"))], inc="sE")
        p.emit("scalar", "activation", Ar, veps, AF.Sqrt,
               waits=[("sE", e)], inc="sD")
        p.emit("vector", "reciprocal", Br, Ar, waits=[("sD", p.c("sD"))])
        vdrain()
        p.emit("vector", "tensor_tensor", Ar, Br, Br, op=OP.mult)
        vdrain()
        p.emit("vector", "tensor_tensor", out, Ar, veps, op=OP.mult, waits=wout)
        vdrain()
        p.emit("vector", "tensor_scalar", Ar, out, -0.5, 1.5, op0=OP.mult, op1=OP.add)
        vdrain()
        p.emit("vector", "tensor_tensor", out, Br, Ar, op=OP.mult, inc="sE")
        return e

    p.emit("gpsimd", "dma_start", out=IDN[:, :], in_=d_id[:, :], inc="sA", incn=16)
    p.emit("gpsimd", "dma_start", out=GQ[:, :], in_=d_gq[:, :], inc="sA", incn=16)
    p.emit("gpsimd", "dma_start", out=GK[:, :], in_=d_gk[:, :], inc="sA", incn=16)
    p.emit("sync", "dma_start",
           out=fsl(WKS, 4096).rearrange("p (a b) -> p a b", a=32),
           in_=d_wk.rearrange("p fc dc f -> p (fc dc) f"), inc="sB", incn=16)
    p.emit("sync", "dma_start",
           out=fsl(WVS, 4096).rearrange("p (a b) -> p a b", a=16),
           in_=d_wv[:, :, :], inc="sB", incn=16)
    p.emit("vector", "memset", ONE[:, :], 1.0)
    p.emit("vector", "memset", ONR[:, :], 1.0, inc="sE")
    const_sE = p.c("sE")
    const_sA = p.c("sA")
    const_sB = p.c("sB")

    xslot_free = [("sE", 0), ("sE", 0)]
    xnt_free = [("sC", 0), ("sC", 0)]
    tbank_free = [("sD", 0), ("sD", 0)]
    kbank_free = [("sE", 0), ("sE", 0)]
    vbank_free = [("sD", 0), ("sD", 0)]
    cks_free = ("sE", 0)
    row4_free = ("sE", 0)
    krw_war = ("sD", 0)
    bc5_war = ("sC", 0)

    def xn_view(dcg, ci):
        v = fsl(XF, 8192).rearrange("p (dc t) -> p dc t", dc=16)
        return v[:, 4 * dcg:4 * dcg + 4, ci * 128:(ci + 1) * 128]

    def ln_chunk(src_ap, slot, ci):
        xb = fsl(XB + slot * 2048, 2048)
        xnt = fsl(XNT + slot * 2048, 2048)
        a = p.emit("gpsimd", "dma_start", out=xb, in_=src_ap,
                   waits=[xslot_free[slot], ("sA", const_sA)], inc="sA", incn=16)
        wA = [("sA", a)]
        p.emit("vector", "tensor_reduce", ST[:, 0:1], xb, axis=AX.X, op=OP.add,
               waits=wA)
        vdrain()
        p.emit("scalar", "activation", fsl(XSQ, 2048), xb, AF.Square,
               accum_out=ST[:, 1:2], waits=wA, inc="sD")
        wsq = [("sD", p.c("sD"))]
        p.emit("vector", "tensor_scalar", ST[:, 2:3], ST[:, 0:1], 1.0 / D, None,
               op0=OP.mult)
        p.emit("vector", "tensor_scalar", ST[:, 3:4], ST[:, 1:2], 1.0 / D, LN_EPS,
               op0=OP.mult, op1=OP.add, waits=wsq)
        vdrain()
        p.emit("vector", "tensor_tensor", ST[:, 4:5], ST[:, 2:3], ST[:, 2:3], op=OP.mult)
        vdrain()
        p.emit("vector", "tensor_tensor", ST[:, 5:6], ST[:, 3:4], ST[:, 4:5],
               op=OP.subtract, inc="sE")
        rsqrt_cols(5, [("sE", p.c("sE"))])
        vdrain()
        p.emit("vector", "tensor_tensor", ST[:, 12:13], ST[:, 2:3], ST[:, 11:12], op=OP.mult)
        vdrain()
        p.emit("vector", "tensor_scalar", ST[:, 13:14], ST[:, 12:13], -1.0, None,
               op0=OP.mult)
        p.emit("vector", "drain")
        e = p.emit("vector", "tensor_scalar", xnt, xb, ST[:, 11:12], ST[:, 13:14],
                   op0=OP.mult, op1=OP.add, waits=[xnt_free[slot]], inc="sE")
        xslot_free[slot] = ("sE", e)
        for g in range(4):
            tb = g % 2
            for d4 in range(4):
                w = []
                if d4 == 0:
                    w = [("sE", e), tbank_free[tb]]
                p.emit("tensor", "transpose", bank(tb)[:, d4 * 128:(d4 + 1) * 128],
                       xnt[:, (4 * g + d4) * 128:(4 * g + d4 + 1) * 128], IDN[:, :],
                       waits=w, inc="sC" if d4 == 3 else None)
            c = p.c("sC")
            dv = p.emit("scalar", "copy", xn_view(g, ci),
                        bank(tb).rearrange("p (a b) -> p a b", a=4),
                        waits=[("sC", c)], inc="sD")
            tbank_free[tb] = ("sD", dv)
            if g == 3:
                xnt_free[slot] = ("sC", c)

    for tile in range(8):
        for ci in range(4):
            ln_chunk(d_xs[(tile * 4 + ci) * 128:(tile * 4 + ci + 1) * 128, :],
                     (tile * 4 + ci) % 2, ci)
        xn_ready = p.c("sD")
        for fc in range(2):
            for dc in range(16):
                w = []
                if dc == 0:
                    w = [("sD", xn_ready), ("sB", const_sB), kbank_free[fc]]
                mm(bank(2 + fc), fsl(WKS + (fc * 16 + dc) * 128, 128),
                   fsl(XF + dc * 512, 512), dc == 0, dc == 15, waits=w,
                   inc="sC" if dc == 15 else None)
        kps = p.c("sC")
        for fc in range(2):
            p.emit("vector", "tensor_copy", fsl(KRW + fc * 512, 512), bank(2 + fc),
                   waits=[("sC", kps), krw_war] if fc == 0 else [], inc="sE")
        kraw_e = p.c("sE")
        kbank_free[0] = kbank_free[1] = ("sE", kraw_e)
        for fc in range(2):
            p.emit("scalar", "activation", fsl(KSQ + fc * 512, 512),
                   fsl(KRW + fc * 512, 512), AF.Square,
                   waits=[("sE", kraw_e)] if fc == 0 else [], inc="sD")
        ksq_d = p.c("sD")
        krw_war = ("sD", ksq_d)
        for fc in range(2):
            mm(bankrow(4), ONE[:, :], fsl(KSQ + fc * 512, 512), fc == 0, fc == 1,
               waits=[("sD", ksq_d), row4_free] if fc == 0 else [],
               inc="sC" if fc == 1 else None)
        ve = rsqrt_row(bankrow(4), 512, 1.0 / HD, RMS_EPS, [bc5_war], RWB1)
        row4_free = ("sE", ve)
        rrow_e = p.c("sE")
        bc = mm(bank(5), ONR[:, :], F[0:1, RWB1:RWB1 + 512], True, True,
                waits=[("sE", rrow_e), ("sE", const_sE)], inc="sC")
        bc5_war = ("sC", bc)
        ca = p.emit("gpsimd", "dma_start", out=fsl(CKS, 512),
                    in_=d_ck[:, tile * 512:(tile + 1) * 512],
                    waits=[cks_free], inc="sA", incn=16)
        ca = p.emit("gpsimd", "dma_start", out=fsl(SKS, 512),
                    in_=d_sk[:, tile * 512:(tile + 1) * 512], inc="sA", incn=16)
        for fc in range(2):
            p.emit("vector", "tensor_tensor", fsl(KSQ + fc * 512, 512),
                   fsl(KRW + fc * 512, 512), bank(5), op=OP.mult,
                   waits=[("sC", bc)] if fc == 0 else [])
            p.emit("vector", "tensor_scalar", fsl(KRW + fc * 512, 512),
                   fsl(KSQ + fc * 512, 512), GK[:, fc:fc + 1], None, op0=OP.mult)
        k0 = fsl(KRW, 512); k1 = fsl(KRW + 512, 512)
        t0 = fsl(KSQ, 512); t1 = fsl(KSQ + 512, 512)
        ck = fsl(CKS, 512); sk = fsl(SKS, 512)
        p.emit("vector", "tensor_tensor", t0, k0, ck, op=OP.mult, waits=[("sA", ca)])
        p.emit("vector", "tensor_tensor", t1, k1, sk, op=OP.mult)
        p.emit("vector", "tensor_tensor", fsl(KB + tile * 512, 512), t0, t1,
               op=OP.subtract)
        p.emit("vector", "tensor_tensor", t0, k0, sk, op=OP.mult)
        e = p.emit("vector", "tensor_tensor", t1, k1, ck, op=OP.mult, inc="sE")
        cks_free = ("sE", e)
        p.emit("vector", "tensor_tensor", fsl(KB + 4096 + tile * 512, 512), t0, t1,
               op=OP.add, inc="sE")
        for ci in range(4):
            vb = 6 + ci % 2
            for dc in range(16):
                w = []
                if dc == 0:
                    w = [("sD", xn_ready), ("sB", const_sB), vbank_free[ci % 2]]
                mm(bank(vb, 256), fsl(XF + dc * 512 + ci * 128, 128),
                   fsl(WVS + dc * 256, 256), dc == 0, dc == 15, waits=w,
                   inc="sC" if dc == 15 else None)
            dv = p.emit("scalar", "copy",
                        fsl(VB + (tile * 4 + ci) * 256, 256), bank(vb, 256),
                        waits=[("sC", p.c("sC"))], inc="sD")
            vbank_free[ci % 2] = ("sD", dv)

    for tile in range(2):
        for ci in range(4):
            tc = tile * 4 + ci
            ln_chunk(d_xq[tc * 128:(tc + 1) * 128, :], tc % 2, ci)
        rd = p.c("sD")
        for ci in range(4):
            tc = tile * 4 + ci
            v = fsl(XF, 8192).rearrange("p (dc t) -> p dc t", dc=16)
            p.emit("gpsimd", "dma_start", out=d_xnsp[tc],
                   in_=v[:, :, ci * 128:(ci + 1) * 128],
                   waits=[("sD", rd)] if ci == 0 else [], inc="sG", incn=16)
        p.emit("tensor", "nop", waits=[("sG", p.c("sG"))])

    p.barrier()

    if phases < 2:
        return _finish(nc, es, p, sems, dump, F, d_xnsp)
    p.emit("gpsimd", "dma_start", out=fsl(CQF, 1024), in_=d_cq[:, :], inc="sA", incn=16)
    p.emit("gpsimd", "dma_start", out=fsl(SQF, 1024), in_=d_sq[:, :], inc="sA", incn=16)
    cq_a = p.c("sA")
    wq_free = [("bar", 0), ("bar", 0)]
    qbank_free = [("sE", 0), ("sE", 0)]
    row2_free = ("sE", 0)
    bc3_war = ("sC", 0)
    qrw_war = ("sD", 0)

    for half in range(2):
        xa = 0
        for k4 in range(4):
            v = fsl(XNH, 8192).rearrange("p (dc t) -> p dc t", dc=16)
            xa = p.emit("gpsimd", "dma_start", out=v[:, :, k4 * 128:(k4 + 1) * 128],
                        in_=d_xnsp[half * 4 + k4],
                        waits=[("sC", p.c("sC"))] if k4 == 0 else [],
                        inc="sA", incn=16)
        for h in range(H):
            bslots = []
            for i in range(2):
                b = p.emit("sync", "dma_start",
                           out=fsl(WQS + i * 2048, 2048).rearrange(
                               "p (dc f) -> p dc f", dc=16),
                           in_=d_wq[2 * h + i], waits=[wq_free[i]],
                           inc="sB", incn=16)
                bslots.append(b)
            for i in range(2):
                for dc in range(16):
                    w = []
                    if dc == 0:
                        w = [("sB", bslots[i]), ("sA", xa), qbank_free[i]]
                    mm(bank(i), fsl(WQS + i * 2048 + dc * 128, 128),
                       fsl(XNH + dc * 512, 512), dc == 0, dc == 15, waits=w,
                       inc="sC" if dc == 15 else None)
                wq_free[i] = ("sC", p.c("sC"))
            qmm = p.c("sC")
            for i in range(2):
                p.emit("vector", "tensor_copy", fsl(QRW + i * 512, 512), bank(i),
                       waits=[("sC", qmm), qrw_war] if i == 0 else [], inc="sE")
            qr = p.c("sE")
            qbank_free[0] = qbank_free[1] = ("sE", qr)
            for i in range(2):
                p.emit("scalar", "activation", fsl(QS2 + i * 512, 512),
                       fsl(QRW + i * 512, 512), AF.Square,
                       waits=[("sE", qr)] if i == 0 else [], inc="sD")
            qsqd = p.c("sD")
            qrw_war = ("sD", qsqd)
            for i in range(2):
                mm(bankrow(2), ONE[:, :], fsl(QS2 + i * 512, 512), i == 0, i == 1,
                   waits=[("sD", qsqd), row2_free] if i == 0 else [],
                   inc="sC" if i == 1 else None)
            ve = rsqrt_row(bankrow(2), 512, 1.0 / HD, RMS_EPS, [bc3_war], RWB2)
            row2_free = ("sE", ve)
            bc = mm(bank(3), ONR[:, :], F[0:1, RWB2:RWB2 + 512], True, True,
                    waits=[("sE", p.c("sE"))], inc="sC")
            bc3_war = ("sC", bc)
            for i in range(2):
                fc = 2 * h + i
                p.emit("vector", "tensor_tensor", fsl(QS2 + i * 512, 512),
                       fsl(QRW + i * 512, 512), bank(3), op=OP.mult,
                       waits=[("sC", bc), ("sA", cq_a)] if i == 0 else [])
                p.emit("vector", "tensor_scalar", fsl(QRW + i * 512, 512),
                       fsl(QS2 + i * 512, 512), GQ[:, fc:fc + 1], None, op0=OP.mult)
            q0 = fsl(QRW, 512); q1 = fsl(QRW + 512, 512)
            t0 = fsl(QS2, 512); t1 = fsl(QS2 + 512, 512)
            cq = fsl(CQF + half * 512, 512); sq = fsl(SQF + half * 512, 512)
            o0 = fsl(QB + h * 1024 + half * 512, 512)
            o1 = fsl(QB + 8192 + h * 1024 + half * 512, 512)
            p.emit("vector", "tensor_tensor", t0, q0, cq, op=OP.mult)
            p.emit("vector", "tensor_tensor", t1, q1, sq, op=OP.mult)
            p.emit("vector", "tensor_tensor", o0, t0, t1, op=OP.subtract)
            p.emit("vector", "tensor_tensor", t0, q0, sq, op=OP.mult)
            p.emit("vector", "tensor_tensor", t1, q1, cq, op=OP.mult)
            p.emit("vector", "tensor_tensor", o1, t0, t1, op=OP.add, inc="sE")

    p.barrier()

    if phases < 3:
        return _finish(nc, es, p, sems, dump, F, d_xnsp)
    sbank_free = [("bar", 0), ("bar", 0)]
    probs_free = [("bar", 0), ("bar", 0)]
    cbank_free = ("bar", 0)
    row4_free = ("sE", 0)
    wo_free = [("bar", 0), ("bar", 0)]
    otb_free = [("bar", 0), ("bar", 0)]
    b5war = ("bar", 0)
    b5_reader = ("sD", 0)
    otb_prev_qc = ("bar", 0)

    for qc in range(NQC):
        kext = KEXT[qc]
        ma = p.emit("gpsimd", "dma_start",
                    out=fsl(MSK, 2048).rearrange("p (k q) -> p k q", k=8),
                    in_=d_mask[qc], waits=[("sE", p.c("sE"))], inc="sA", incn=16)
        xq_a = 0
        for i in range(2):
            xq_a = p.emit("gpsimd", "dma_start", out=fsl(XQT + i * 2048, 2048),
                          in_=d_xq[qc * 256 + i * 128: qc * 256 + (i + 1) * 128, :],
                          waits=[("sC", p.c("sC"))] if i == 0 else [],
                          inc="sA", incn=16)
        for hp in range(H // 2):
            for kc in range(kext):
                sb = kc % 2
                for fc in range(2):
                    w = []
                    if fc == 0:
                        w = [sbank_free[sb]]
                        if kc == 0:
                            w += [cbank_free, row4_free]
                    rhs = fsl(QB + fc * 8192 + 2 * hp * 1024, 2048).rearrange(
                        "p (h t) -> p h t", h=2)[:, :, qc * 256:(qc + 1) * 256]
                    mm(bank(sb, 512), fsl(KB + fc * 4096 + kc * 128, 128),
                       rhs, fc == 0, fc == 1, waits=w,
                       inc="sC" if fc == 1 else None)
                sc = p.c("sC")
                prb = fsl(PRB + sb * 512, 512)
                d = p.emit("scalar", "activation", prb, bank(sb, 512), AF.Exp,
                           scale=1.0 / 16.0,
                           waits=[("sC", sc), probs_free[sb]], inc="sD")
                sbank_free[sb] = ("sD", d)
                pw = ("sD", d)
                if kc >= kext - NMASK:
                    for hh in range(2):
                        e = p.emit("vector", "tensor_tensor",
                                   fsl(PRB + sb * 512 + hh * 256, 256),
                                   fsl(PRB + sb * 512 + hh * 256, 256),
                                   fsl(MSK + (kc - (kext - NMASK)) * 256, 256),
                                   op=OP.mult,
                                   waits=[("sD", d), ("sA", ma)] if hh == 0 else [],
                                   inc="sE" if hh == 1 else None)
                    pw = ("sE", e)
                last = kc == kext - 1
                mm(bank(2, 512), fsl(VB + kc * 256, 128), prb, kc == 0, last,
                   waits=[pw])
                mm(bank(3, 512), fsl(VB + kc * 256 + 128, 128), prb, kc == 0, last)
                pe = mm(bankrow(4, 512), ONE[:, :], prb, kc == 0, last, inc="sC")
                probs_free[sb] = ("sC", pe)
            re = p.emit("vector", "reciprocal", F[0:1, ROW3:ROW3 + 512],
                        bankrow(4, 512),
                        waits=[("sC", p.c("sC")), bc5_war], inc="sE")
            row4_free = ("sE", re)
            bc = mm(bank(5, 512), ONR[:, :], F[0:1, ROW3:ROW3 + 512], True, True,
                    waits=[("sE", re), b5war], inc="sC")
            bc5_war = ("sC", bc)
            d = p.emit("scalar", "copy", fsl(RBS, 512), bank(5, 512),
                       waits=[("sC", bc), cbank_free], inc="sD")
            b5_reader = ("sD", d)
            for i in range(2):
                for hh in range(2):
                    fc = 4 * hp + 2 * hh + i
                    p.emit("vector", "tensor_tensor",
                           fsl(CTX + fc * 256, 256),
                           bank(2 + i, 512)[:, hh * 256:(hh + 1) * 256],
                           fsl(RBS + hh * 256, 256), op=OP.mult,
                           waits=[("sD", d)] if (i == 0 and hh == 0) else [],
                           inc="sE" if (i == 1 and hh == 1) else None)
            cbank_free = ("sE", p.c("sE"))
        ctx_e = p.c("sE")
        first_mm_done = False
        for fc in range(16):
            slot = fc % 2
            b = p.emit("sync", "dma_start",
                       out=fsl(WOS + slot * 2048, 2048),
                       in_=d_wo[fc], waits=[wo_free[slot]], inc="sB", incn=16)
            for qsub in range(2):
                for og in range(4):
                    bk = qsub * 4 + og
                    w = []
                    if qsub == 0 and og == 0:
                        w = [("sB", b)]
                        if fc == 0:
                            w += [("sE", ctx_e), cbank_free, b5_reader,
                                  sbank_free[0], sbank_free[1], row4_free,
                                  otb_prev_qc]
                    mm(bank(bk, 512), fsl(CTX + fc * 256 + qsub * 128, 128),
                       fsl(WOS + slot * 2048 + og * 512, 512),
                       fc == 0, fc == 15, waits=w,
                       inc="sC" if (qsub == 1 and og == 3) else None)
            wo_free[slot] = ("sC", p.c("sC"))
        omm = p.c("sC")
        for qsub in range(2):
            for og in range(4):
                slot = (qsub * 4 + og) % 2
                w = [otb_free[slot]]
                if qsub == 0 and og == 0:
                    w += [("sC", omm), ("sA", xq_a)]
                e = p.emit("vector", "tensor_tensor", fsl(OTB + slot * 512, 512),
                           bank(qsub * 4 + og, 512),
                           fsl(XQT + qsub * 2048 + og * 512, 512), op=OP.add,
                           waits=w, inc="sE")
                g = p.emit("gpsimd", "dma_start",
                           out=d_out[qc * 256 + qsub * 128: qc * 256 + (qsub + 1) * 128,
                                     og * 512:(og + 1) * 512],
                           in_=fsl(OTB + slot * 512, 512), waits=[("sE", e)],
                           inc="sG", incn=16)
                otb_free[slot] = ("sG", g)
        allfree = ("sE", p.c("sE"))
        otb_prev_qc = allfree
        sbank_free = [allfree, allfree]
        cbank_free = allfree
        row4_free = allfree
        b5war = allfree

    p.barrier()

    if phases < 4:
        return _finish(nc, es, p, sems, dump, F, d_xnsp)
    store_before_p4 = p.c("sG")
    cP4 = p.c("sC"); eP4 = p.c("sE"); dP4 = p.c("sD")

    xnf3 = XNFa.rearrange("p (dc t) -> p dc t", dc=16)
    xa = 0
    for tc in range(8):
        w = ([("sE", eP4), ("sC", cP4), ("sG", p.c("sG")), ("sD", dP4)]
             if tc == 0 else [])
        xa = p.emit("gpsimd", "dma_start", out=xnf3[:, :, tc * 128:(tc + 1) * 128],
                    in_=d_xnsp[tc].rearrange("p (dc t) -> p dc t", dc=16),
                    waits=w, inc="sA", incn=16)
    for q in range(4):
        p.emit("scalar", "activation", XNHa[:, q * 4096:(q + 1) * 4096],
               XNFa[:, q * 4096:(q + 1) * 4096], AF.Copy,
               waits=[("sA", xa)] if q == 0 else [],
               inc="sD" if q == 3 else None)
    xnh_d = p.c("sD")
    for q in range(4):
        p.emit("vector", "tensor_tensor", XNLa[:, q * 4096:(q + 1) * 4096],
               XNFa[:, q * 4096:(q + 1) * 4096], XNHa[:, q * 4096:(q + 1) * 4096],
               op=OP.subtract, waits=[("sD", xnh_d)] if q == 0 else [],
               inc="sE" if q == 3 else None)
    xn_split_e = p.c("sE")

    wf_free = [("bar", 0)] * 3
    wr_free = [("bar", 0)] * NWR
    gu_bank_free = [("bar", 0)] * 8
    hq_free = [("bar", 0), ("bar", 0)]
    spill_g = ("bar", 0)
    slab_e = {}
    bank_stop = [0] * 8

    def gu_slab(c, dc):
        gs = c * 16 + dc
        wfs = gs % 3
        wrs = gs % NWR
        wdma = [wf_free[wfs]]
        if gs == 0:
            wdma += [("sC", cP4), ("sE", eP4)]
        b = p.emit("sync", "dma_start", out=WFa[:, wfs * 1024:(wfs + 1) * 1024],
                   in_=d_wgu[c, dc], waits=wdma, inc="sB", incn=16)
        hi = WRa[:, wrs * 2048:wrs * 2048 + 1024]
        lo = WRa[:, wrs * 2048 + 1024:wrs * 2048 + 2048]
        whl = [("sB", b), wr_free[wrs]]
        if gs == 0:
            whl.append(("sE", xn_split_e))
        p.emit("scalar", "activation", hi, WFa[:, wfs * 1024:(wfs + 1) * 1024],
               AF.Copy, waits=whl, inc="sD")
        e = p.emit("vector", "tensor_tensor", lo,
                   WFa[:, wfs * 1024:(wfs + 1) * 1024], hi, op=OP.subtract,
                   waits=[("sD", p.c("sD"))], inc="sE")
        wf_free[wfs] = ("sE", e)
        slab_e[(c, dc)] = e

    for dc in range(16):
        gu_slab(0, dc)
    for c in range(16):
        for t2 in range(2):
            for dc in range(16):
                wrs = (c * 16 + dc) % NWR
                whi = WRa[:, wrs * 2048:wrs * 2048 + 1024]
                wlo = WRa[:, wrs * 2048 + 1024:wrs * 2048 + 2048]
                xhi = XNHa[:, dc * 1024 + t2 * 512:dc * 1024 + t2 * 512 + 512]
                xlo = XNLa[:, dc * 1024 + t2 * 512:dc * 1024 + t2 * 512 + 512]
                for bk in range(8):
                    j, gu = bk // 2, bk % 2
                    lh_hi = whi[:, (gu * 4 + j) * 128:(gu * 4 + j + 1) * 128]
                    lh_lo = wlo[:, (gu * 4 + j) * 128:(gu * 4 + j + 1) * 128]
                    for term in range(3):
                        lh = lh_hi if term < 2 else lh_lo
                        rh = xlo if term == 1 else xhi
                        w = []
                        if dc == 0 and term == 0:
                            w.append(gu_bank_free[bk])
                        if bk == 0 and term == 0:
                            w.append(("sE", slab_e[(c, dc)]))
                        cnt = mm(bank(bk, 512), lh, rh,
                                 dc == 0 and term == 0,
                                 dc == 15 and term == 2, waits=w,
                                 inc="sC" if term == 2 and (bk == 7 or dc == 15)
                                 else None)
                        if dc == 15 and term == 2:
                            bank_stop[bk] = cnt
                if t2 == 1:
                    wr_free[wrs] = ("sC", p.c("sC"))
            for j in range(4):
                gb = bank(j * 2, 512)
                ub = bank(j * 2 + 1, 512)
                hq = HQa[:, (c % 2) * 4096 + j * 1024 + t2 * 512:(c % 2) * 4096 + j * 1024 + t2 * 512 + 512]
                wsl = [("sC", bank_stop[j * 2])]
                if t2 == 0 and j == 0:
                    wsl.append(hq_free[c % 2])
                d = p.emit("scalar", "activation", hq, gb, AF.Silu, waits=wsl,
                           inc="sD")
                e = p.emit("vector", "tensor_tensor", hq, hq, ub, op=OP.mult,
                           waits=[("sD", d), ("sC", bank_stop[j * 2 + 1])],
                           inc="sE")
                gu_bank_free[j * 2] = ("sD", d)
                gu_bank_free[j * 2 + 1] = ("sE", e)
        hall = p.c("sE")
        if c < 15:
            for dc in range(16):
                gu_slab(c + 1, dc)
        base = (c % 2) * 4096
        J = [HQa[:, base + j * 1024:base + (j + 1) * 1024] for j in range(4)]
        A, Bb, C, Dd, E = [SPa[:, i * 1024:(i + 1) * 1024] for i in range(5)]
        p.emit("scalar", "activation", A, J[0], AF.Abs,
               waits=[("sE", hall), spill_g])
        p.emit("scalar", "activation", Bb, J[1], AF.Abs)
        p.emit("scalar", "activation", C, J[2], AF.Abs)
        p.emit("scalar", "activation", Dd, J[3], AF.Abs, inc="sD")
        wd0 = [("sD", p.c("sD"))]
        p.emit("vector", "tensor_tensor", E, A, Bb, op=OP.max, waits=wd0)
        p.emit("vector", "tensor_tensor", A, A, Bb, op=OP.min)
        p.emit("vector", "tensor_tensor", Bb, C, Dd, op=OP.max)
        p.emit("vector", "tensor_tensor", C, C, Dd, op=OP.min)
        p.emit("vector", "tensor_tensor", E, E, Bb, op=OP.min)
        p.emit("vector", "tensor_tensor", A, A, C, op=OP.max)
        p.emit("vector", "tensor_tensor", E, E, A, op=OP.max, inc="sE")
        p.emit("scalar", "activation", A, J[0], AF.Abs,
               waits=[("sE", p.c("sE"))])
        p.emit("scalar", "activation", Bb, J[1], AF.Abs)
        p.emit("scalar", "activation", C, J[2], AF.Abs, inc="sD")
        wd1 = [("sD", p.c("sD"))]
        for j, src in enumerate((A, Bb, C, Dd)):
            p.emit("vector", "tensor_tensor", src, src, E, op=OP.is_ge,
                   waits=wd1 if j == 0 else [])
            p.emit("vector", "tensor_tensor", J[j], J[j], src, op=OP.mult,
                   inc="sE" if j == 3 else None)
        spars_e = p.c("sE")
        d = p.emit("scalar", "activation", HQHa[:, :], HQa[:, base:base + 4096],
                   AF.Copy, waits=[("sE", spars_e)], inc="sD")
        e = p.emit("vector", "tensor_tensor", HQLa[:, :], HQa[:, base:base + 4096],
                   HQHa[:, :], op=OP.subtract, waits=[("sD", d)],
                   inc="sE")
        hq_free[c % 2] = ("sE", e)
        p.emit("gpsimd", "dma_start", out=d_hsp[c][:, 0, :],
               in_=HQHa[:, :], waits=[("sD", d)], inc="sG", incn=16)
        g2 = p.emit("gpsimd", "dma_start", out=d_hsp[c][:, 1, :],
                    in_=HQLa[:, :], waits=[("sE", e)], inc="sG", incn=16)
        spill_g = ("sG", g2)

    gu_end = p.c("sC")
    spill_all = p.c("sG")
    eP4c = p.c("sE")
    dP4c = p.c("sD")
    wdf_free = [("bar", 0), ("bar", 0)]
    wdb_free = [("bar", 0)] * 4
    dbank_free = list(gu_bank_free)
    rbt_free = [("bar", 0), ("bar", 0)]
    otf_free = [("bar", 0), ("bar", 0)]
    hrd_war = [("bar", 0), ("bar", 0)]
    ha = [None, None, None, None]

    def hrd_load(qt):
        hb = (qt % 2) * 32768
        d = {}
        for ic in range(64):
            cq_, jq = ic % 16, ic // 16
            w = []
            if ic == 0:
                w = [("sG", spill_all), hrd_war[qt % 2]]
                if qt == 0:
                    w.append(("sC", gu_end))
            d[ic] = p.emit("gpsimd", "dma_start",
                           out=HRDa[:, hb + ic * 512:hb + ic * 512 + 512].rearrange(
                               "p (a b) -> p a b", a=2),
                           in_=d_hsp[cq_][:, :, jq * 1024 + qt * 256:
                                          jq * 1024 + qt * 256 + 256],
                           waits=w, inc="sA", incn=16)
        ha[qt] = d

    hrd_load(0)
    for qt in range(4):
        hb = (qt % 2) * 32768
        for ic in range(64):
            wfs = ic % 2
            wbs = ic % 4
            w = [wdf_free[wfs]]
            if qt == 0 and ic == 0:
                w += [("sE", eP4c), ("sD", dP4c)]
            b = p.emit("sync", "dma_start", out=WDFa[:, wfs * 2048:(wfs + 1) * 2048],
                       in_=d_wd[ic], waits=w, inc="sB", incn=16)
            whh = WDBa[:, wbs * 4096:wbs * 4096 + 2048]
            wll = WDBa[:, wbs * 4096 + 2048:wbs * 4096 + 4096]
            whl = [("sB", b), wdb_free[wbs]]
            if qt == 0 and ic == 0:
                whl.append(("sG", spill_all))
            d = p.emit("scalar", "activation", whh,
                       WDFa[:, wfs * 2048:(wfs + 1) * 2048], AF.Copy, waits=whl,
                       inc="sD")
            e = p.emit("vector", "tensor_tensor", wll,
                       WDFa[:, wfs * 2048:(wfs + 1) * 2048], whh, op=OP.subtract,
                       waits=[("sD", p.c("sD"))], inc="sE")
            wdf_free[wfs] = ("sE", e)
            hh = HRDa[:, hb + ic * 512:hb + ic * 512 + 256]
            hl = HRDa[:, hb + ic * 512 + 256:hb + ic * 512 + 512]
            for tch in range(2):
                for dg in range(4):
                    bk = tch * 4 + dg
                    for term in range(3):
                        lh = (hh if term < 2 else hl)[:, tch * 128:(tch + 1) * 128]
                        rh = (whh if term != 1 else wll)[:, dg * 512:(dg + 1) * 512]
                        w = []
                        if term == 0 and tch == 0 and dg == 0:
                            w = [("sE", e), ("sA", ha[qt][ic])]
                        if term == 0 and ic == 0:
                            w.append(dbank_free[bk])
                        cnt = mm(bank(bk, 512), lh, rh,
                                 ic == 0 and term == 0,
                                 ic == 63 and term == 2, waits=w,
                                 inc="sC" if term == 2 and (bk == 7 or ic == 63)
                                 else None)
            wdb_free[wbs] = ("sC", p.c("sC"))
        dmm = p.c("sC")
        hrd_war[qt % 2] = ("sC", dmm)
        if qt < 3:
            hrd_load(qt + 1)
        for tch in range(2):
            for dg in range(4):
                slot = (tch * 4 + dg) % 2
                w = [("sG", store_before_p4), ("sC", dmm), rbt_free[slot]]
                if qt == 0 and tch == 0 and dg == 0:
                    w += [("sE", p.c("sE")), ("sD", p.c("sD"))]
                g = p.emit("gpsimd", "dma_start", out=RBTa[:, slot * 512:(slot + 1) * 512],
                           in_=d_out[qt * 256 + tch * 128: qt * 256 + (tch + 1) * 128,
                                     dg * 512:(dg + 1) * 512],
                           waits=w, inc="sR", incn=16)
                e2 = p.emit("vector", "tensor_tensor", OTFa[:, slot * 512:(slot + 1) * 512],
                            bank(tch * 4 + dg, 512), RBTa[:, slot * 512:(slot + 1) * 512],
                            op=OP.add, waits=[("sR", g), otf_free[slot]], inc="sE")
                dbank_free[tch * 4 + dg] = ("sE", e2)
                rbt_free[slot] = ("sE", e2)
                g2 = p.emit("gpsimd", "dma_start",
                            out=d_out[qt * 256 + tch * 128: qt * 256 + (tch + 1) * 128,
                                      dg * 512:(dg + 1) * 512],
                            in_=OTFa[:, slot * 512:(slot + 1) * 512], waits=[("sE", e2)],
                            inc="sG", incn=16)
                otf_free[slot] = ("sG", g2)

    p.barrier()
    return _finish(nc, es, p, sems, dump, F, d_xnsp)


def _finish(nc, es, p, sems, dump=False, F=None, d_xnsp=None):
    if dump:
        d_dbg = nc.dram_tensor("dbgF", [128, POOL], F32, kind="ExternalOutput").ap()
        d_dbg2 = nc.dram_tensor("dbgsp", [8, 128, D], F32, kind="ExternalOutput").ap()
        dbgb = es.enter_context(nc.sbuf_tensor("dbgb", [128, 256], F32)).ap()
        nch = (POOL + 4095) // 4096
        for i in range(nch):
            n = min(4096, POOL - i * 4096)
            p.emit("gpsimd", "dma_start", out=d_dbg[:, i*4096:i*4096+n],
                   in_=F[:, i*4096:i*4096+n], inc="sG", incn=16)
        for tc in range(8):
            for i in range(8):
                p.emit("gpsimd", "dma_start", out=dbgb[:, :],
                       in_=d_xnsp[tc][:, i*256:(i+1)*256],
                       waits=[("sG", p.c("sG"))], inc="sA", incn=16)
                p.emit("gpsimd", "dma_start", out=d_dbg2[tc][:, i*256:(i+1)*256],
                       in_=dbgb[:, :], waits=[("sA", p.c("sA"))], inc="sG", incn=16)
        p.emit("gpsimd", "nop", waits=[("sG", p.c("sG"))])
    with es, nc.Block() as block:
        def mk(engname):
            def body(eng):
                for op, args, kwargs, ws, inc, incn in p.ops[engname]:
                    for (sn, val) in ws:
                        eng.wait_ge(sems[sn], val)
                    inst = getattr(eng, op)(*args, **kwargs)
                    if inc is not None:
                        inst.then_inc(sems[inc], incn)
            return body
        block.tensor(mk("tensor"))
        block.vector(mk("vector"))
        block.scalar(mk("scalar"))
        block.gpsimd(mk("gpsimd"))
        block.sync(mk("sync"))
    return nc



def _core_inputs(c, x, shared, ckf_h, skf_h):
    seq, j = c // 4, c % 4
    chunks = [j, 7 - j, 8 + j, 15 - j]
    own_idx = np.concatenate([np.arange(ch * 256, (ch + 1) * 256) for ch in chunks])
    xs = np.ascontiguousarray(x[seq])
    xq = np.ascontiguousarray(x[seq][own_idx])
    cqf = np.ascontiguousarray(ckf_h[:, own_idx])
    sqf = np.ascontiguousarray(skf_h[:, own_idx])
    mask = np.zeros((NQC, 128, NMASK, 256), np.float32)
    for qc in range(NQC):
        base = KEXT[qc] * 128 - 1024
        kpos = base + np.arange(1024)
        qpos = own_idx[qc * 256:(qc + 1) * 256]
        m = (kpos[:, None] <= qpos[None, :]).astype(np.float32)
        mask[qc] = m.reshape(NMASK, 128, 256).transpose(1, 0, 2)
    d = {"xs": xs, "xq": xq, "ckf": ckf_h, "skf": skf_h,
         "cqf": cqf, "sqf": sqf, "mask": mask}
    d.update(shared)
    return d, own_idx


def prep_inputs(inputs):
    x = np.asarray(inputs["L0_attn_x"], np.float32)
    q_proj = np.asarray(inputs["L0_attn_q_proj"], np.float32)
    k_proj = np.asarray(inputs["L0_attn_k_proj"], np.float32)
    v_proj = np.asarray(inputs["L0_attn_v_proj"], np.float32)
    o_proj = np.asarray(inputs["L0_attn_o_proj"], np.float32)
    q_norm = np.asarray(inputs["L0_attn_q_norm"], np.float32)
    k_norm = np.asarray(inputs["L0_attn_k_norm"], np.float32)
    w_gate = np.asarray(inputs["L0_ffn_w_gate"], np.float32)
    w_up = np.asarray(inputs["L0_ffn_w_up"], np.float32)
    w_down = np.asarray(inputs["L0_ffn_w_down"], np.float32)

    wq_h = np.ascontiguousarray(q_proj.reshape(16, 128, 16, 128).transpose(0, 3, 2, 1))
    wk_h = np.ascontiguousarray(k_proj.reshape(2, 128, 16, 128).transpose(3, 0, 2, 1))
    wv_h = np.ascontiguousarray(v_proj.reshape(256, 16, 128).transpose(2, 1, 0))
    wo_h = np.ascontiguousarray(o_proj.T.reshape(16, 128, 2048))
    G5 = w_gate.reshape(16, 128, 16, 128, 4).transpose(2, 0, 1, 4, 3)
    U5 = w_up.reshape(16, 128, 16, 128, 4).transpose(2, 0, 1, 4, 3)
    wgu_h = np.ascontiguousarray(
        np.stack([G5, U5], axis=3).reshape(16, 16, 128, 1024))
    wd_h = np.ascontiguousarray(
        w_down.reshape(2048, 4, 2048).transpose(1, 0, 2).reshape(64, 128, 2048))
    gq_h = np.ascontiguousarray(np.tile(q_norm, 8).reshape(16, 128).T)
    gk_h = np.ascontiguousarray(k_norm.reshape(2, 128).T)

    idx = np.arange(128, dtype=np.float64)
    inv = 1.0 / (1e6 ** (2.0 * idx / 256.0))
    pos = np.arange(L, dtype=np.float64)
    fr = np.outer(inv, pos)
    ckf_h = np.cos(fr).astype(np.float32)
    skf_h = np.sin(fr).astype(np.float32)

    shared = {"wq": wq_h, "wk": wk_h, "wv": wv_h, "wo": wo_h, "wgu": wgu_h,
              "wd": wd_h, "gq": gq_h, "gk": gk_h,
              "ident": np.eye(128, dtype=np.float32)}

    in_maps, own_idxs = [], []
    for c in range(NCORE):
        m, oi = _core_inputs(c, x, shared, ckf_h, skf_h)
        in_maps.append(m)
        own_idxs.append(oi)
    return in_maps, own_idxs


def kernel(**inputs):
    in_maps, own_idxs = prep_inputs(inputs)
    nc = build_program()
    res = run_bass_kernel_spmd(nc, in_maps, core_ids=list(range(NCORE)))
    out = np.empty((B, L, D), np.float32)
    for c in range(NCORE):
        out[c // 4, own_idxs[c]] = np.asarray(res.results[c]["out"])
    return out


# version 14 (final)
# speedup vs baseline: 1.3337x; 1.0378x over previous
import contextlib
import numpy as np

import concourse.bass as bass
import concourse.mybir as mybir
from concourse.bass_utils import run_bass_kernel_spmd
from concourse.alu_op_type import AluOpType as OP

F32 = mybir.dt.float32
BF16 = mybir.dt.bfloat16
AF = mybir.ActivationFunctionType
AX = mybir.AxisListType

ENGINES = ("tensor", "vector", "scalar", "gpsimd", "sync")

B, L, D = 2, 4096, 2048
H, HD = 8, 256
NCORE = 8
OWN = 1024
NQC = 4
KEXT = [8, 16, 24, 32]
NMASK = 8
LN_EPS = 1e-5
RMS_EPS = 1e-6

POOL = 51200
KB = 0
VB = 8192
QB = 16384
WKS = 16384
WVS = 20480
XF = 24576
XB = 32768
XSQ = 36864
XNT = 38912
KRW = 43008
KSQ = 44032
CKS = 45056
SKS = 45568
RWB1 = 46080
XNH = 32768
WQS = 40960
QRW = 45056
QS2 = 46080
CQF = 47104
SQF = 48128
RWB2 = 49152
CTX = 32768
PRB = 36864
MSK = 37888
XQT = 39936
RBS = 44032
OTB = 44544
WOS = 45568
ROW3 = 49664
XNH_E = 0
XNL_E = 16384
WR_E = 32768
NWR = 17
XNF_U = 16384
WF_U = 33792
HQ_U = 36864
SP_U = 45056
HQH_E = 90112
HQL_E = 94208
HRD_E = 0
WDB_E = 32768
WDF_U = 33792
RBT_U = 38912
OTF_U = 39936


class Prog:
    def __init__(self):
        self.ops = {e: [] for e in ENGINES}
        self.cnt = {}

    def emit(self, eng, opname, *args, waits=(), inc=None, incn=1, **kwargs):
        ws = [(s, v) for (s, v) in waits if v > 0]
        self.ops[eng].append((opname, args, kwargs, ws, inc, incn))
        if inc is not None:
            self.cnt[inc] = self.cnt.get(inc, 0) + incn
        return self.cnt.get(inc, 0)

    def c(self, sem):
        return self.cnt.get(sem, 0)

    def barrier(self):
        self.emit("gpsimd", "nop",
                  waits=[("sA", self.c("sA")), ("sG", self.c("sG")),
                         ("sR", self.c("sR"))], inc="bar")
        self.emit("sync", "nop", waits=[("sB", self.c("sB"))], inc="bar")
        self.emit("vector", "drain")
        self.emit("scalar", "drain")
        for e in ("tensor", "vector", "scalar"):
            self.emit(e, "nop", inc="bar")
        tgt = self.c("bar")
        for e in ENGINES:
            self.emit(e, "nop", waits=[("bar", tgt)])


def build_program(phases=4, dump=False):
    nc = bass.Bass("TRN2", target_bir_lowering=False, debug=False,
                   detect_race_conditions=False)

    d_xs = nc.dram_tensor("xs", [L, D], F32, kind="ExternalInput").ap()
    d_xq = nc.dram_tensor("xq", [OWN, D], F32, kind="ExternalInput").ap()
    d_ck = nc.dram_tensor("ckf", [128, L], F32, kind="ExternalInput").ap()
    d_sk = nc.dram_tensor("skf", [128, L], F32, kind="ExternalInput").ap()
    d_cq = nc.dram_tensor("cqf", [128, OWN], F32, kind="ExternalInput").ap()
    d_sq = nc.dram_tensor("sqf", [128, OWN], F32, kind="ExternalInput").ap()
    d_mask = nc.dram_tensor("mask", [NQC, 128, NMASK, 256], F32, kind="ExternalInput").ap()
    d_wq = nc.dram_tensor("wq", [16, 128, 16, 128], F32, kind="ExternalInput").ap()
    d_wk = nc.dram_tensor("wk", [128, 2, 16, 128], F32, kind="ExternalInput").ap()
    d_wv = nc.dram_tensor("wv", [128, 16, 256], F32, kind="ExternalInput").ap()
    d_wo = nc.dram_tensor("wo", [16, 128, 2048], F32, kind="ExternalInput").ap()
    d_wgu = nc.dram_tensor("wgu", [16, 16, 128, 1024], F32, kind="ExternalInput").ap()
    d_wd = nc.dram_tensor("wd", [64, 128, 2048], F32, kind="ExternalInput").ap()
    hshape = [4, 16, 128, 4, 2, 256]
    if dump:
        d_hsp = nc.dram_tensor("hsp", hshape, BF16, kind="ExternalOutput")
    else:
        d_hsp = nc.dram_tensor("hsp", hshape, BF16)
    d_hspv = d_hsp.rearrange("qt c p j hl t -> c hl p qt j t")
    d_gq = nc.dram_tensor("gq", [128, 16], F32, kind="ExternalInput").ap()
    d_gk = nc.dram_tensor("gk", [128, 2], F32, kind="ExternalInput").ap()
    d_id = nc.dram_tensor("ident", [128, 128], F32, kind="ExternalInput").ap()
    d_out = nc.dram_tensor("out", [OWN, D], F32, kind="ExternalOutput").ap()
    d_xnsp = nc.dram_tensor("xnsp", [8, 128, D], F32)

    es = contextlib.ExitStack()
    fbase = (nc.sbuf_base + 31) // 32 * 32
    F = es.enter_context(nc.sbuf_tensor("F", [128, POOL], F32)).ap()

    def alias(name, n_elem, unit_off, dt):
        return nc.alloc_sbuf_tensor_at(
            name, [128, n_elem], dt, offset=fbase + 4 * unit_off).ap()

    XNFa = alias("XNFa", 16384, XNF_U, F32)
    WFa = alias("WFa", 3072, WF_U, F32)
    HQa = alias("HQa", 8192, HQ_U, F32)
    SPa = alias("SPa", 5120, SP_U, F32)
    XNHa = alias("XNHa", 16384, 0, BF16)
    XNLa = alias("XNLa", 16384, 8192, BF16)
    WRa = alias("WRa", NWR * 2048, 16384, BF16)
    HQHa = alias("HQHa", 4096, SP_U, BF16)
    HQLa = alias("HQLa", 4096, SP_U + 2048, BF16)
    HRDa = alias("HRDa", 32768, 0, BF16)
    WDBa = alias("WDBa", 16384, 40960, BF16)
    WDFa = alias("WDFa", 4096, WDF_U, F32)
    RBTa = alias("RBTa", 4096, 16384, F32)
    OTFa = alias("OTFa", 4096, 20480, F32)
    IDN = es.enter_context(nc.sbuf_tensor("IDN", [128, 128], F32)).ap()
    ONE = es.enter_context(nc.sbuf_tensor("ONE", [128, 1], F32)).ap()
    ONR = es.enter_context(nc.sbuf_tensor("ONR", [1, 128], F32)).ap()
    GQ = es.enter_context(nc.sbuf_tensor("GQ", [128, 16], F32)).ap()
    GK = es.enter_context(nc.sbuf_tensor("GK", [128, 2], F32)).ap()
    ST = es.enter_context(nc.sbuf_tensor("ST", [128, 16], F32)).ap()
    PS = es.enter_context(nc.psum_tensor("PS", [128, 4096], F32)).ap()
    sems = {}
    for s in ("sA", "sB", "sC", "sD", "sE", "sG", "sR", "bar"):
        sems[s] = es.enter_context(nc.semaphore(s))

    p = Prog()

    def bank(b, n=512):
        return PS[:, b * 512:b * 512 + n]

    def bankrow(b, n=512):
        return PS[0:1, b * 512:b * 512 + n]

    def fsl(off, n):
        return F[:, off:off + n]

    def fsb(off_e, n):
        return Fb[:, off_e:off_e + n]

    def mm(out_ap, lhsT, rhs, start, stop, waits=(), inc=None):
        return p.emit("tensor", "matmul", out_ap, lhsT, rhs, start=start,
                      stop=stop, waits=waits, inc=inc)

    def vdrain():
        p.emit("vector", "drain")

    def rsqrt_cols(vc, wv):
        p.emit("scalar", "activation", ST[:, 6:7], ST[:, vc:vc + 1], AF.Sqrt,
               waits=wv, inc="sD")
        wd = [("sD", p.c("sD"))]
        p.emit("vector", "reciprocal", ST[:, 7:8], ST[:, 6:7], waits=wd)
        vdrain()
        p.emit("vector", "tensor_tensor", ST[:, 8:9], ST[:, 7:8], ST[:, 7:8], op=OP.mult)
        vdrain()
        p.emit("vector", "tensor_tensor", ST[:, 9:10], ST[:, 8:9], ST[:, vc:vc + 1], op=OP.mult)
        vdrain()
        p.emit("vector", "tensor_scalar", ST[:, 10:11], ST[:, 9:10], -0.5, 1.5,
               op0=OP.mult, op1=OP.add)
        vdrain()
        p.emit("vector", "tensor_tensor", ST[:, 11:12], ST[:, 7:8], ST[:, 10:11], op=OP.mult)

    def rsqrt_row(src_row, n, scale, eps, wout, base):
        veps = F[0:1, base + 512:base + 512 + n]
        Ar = F[0:1, base + 1024:base + 1024 + n]
        Br = F[0:1, base + 1536:base + 1536 + n]
        out = F[0:1, base:base + n]
        e = p.emit("vector", "tensor_scalar", veps, src_row, scale, eps,
                   op0=OP.mult, op1=OP.add, waits=[("sC", p.c("sC"))], inc="sE")
        p.emit("scalar", "activation", Ar, veps, AF.Sqrt,
               waits=[("sE", e)], inc="sD")
        p.emit("vector", "reciprocal", Br, Ar, waits=[("sD", p.c("sD"))])
        vdrain()
        p.emit("vector", "tensor_tensor", Ar, Br, Br, op=OP.mult)
        vdrain()
        p.emit("vector", "tensor_tensor", out, Ar, veps, op=OP.mult, waits=wout)
        vdrain()
        p.emit("vector", "tensor_scalar", Ar, out, -0.5, 1.5, op0=OP.mult, op1=OP.add)
        vdrain()
        p.emit("vector", "tensor_tensor", out, Br, Ar, op=OP.mult, inc="sE")
        return e

    p.emit("gpsimd", "dma_start", out=IDN[:, :], in_=d_id[:, :], inc="sA", incn=16)
    p.emit("gpsimd", "dma_start", out=GQ[:, :], in_=d_gq[:, :], inc="sA", incn=16)
    p.emit("gpsimd", "dma_start", out=GK[:, :], in_=d_gk[:, :], inc="sA", incn=16)
    p.emit("sync", "dma_start",
           out=fsl(WKS, 4096).rearrange("p (a b) -> p a b", a=32),
           in_=d_wk.rearrange("p fc dc f -> p (fc dc) f"), inc="sB", incn=16)
    p.emit("sync", "dma_start",
           out=fsl(WVS, 4096).rearrange("p (a b) -> p a b", a=16),
           in_=d_wv[:, :, :], inc="sB", incn=16)
    p.emit("vector", "memset", ONE[:, :], 1.0)
    p.emit("vector", "memset", ONR[:, :], 1.0, inc="sE")
    const_sE = p.c("sE")
    const_sA = p.c("sA")
    const_sB = p.c("sB")

    xslot_free = [("sE", 0), ("sE", 0)]
    xnt_free = [("sC", 0), ("sC", 0)]
    tbank_free = [("sD", 0), ("sD", 0)]
    kbank_free = [("sE", 0), ("sE", 0)]
    vbank_free = [("sD", 0), ("sD", 0)]
    cks_free = ("sE", 0)
    row4_free = ("sE", 0)
    krw_war = ("sD", 0)
    bc5_war = ("sC", 0)

    def xn_view(dcg, ci):
        v = fsl(XF, 8192).rearrange("p (dc t) -> p dc t", dc=16)
        return v[:, 4 * dcg:4 * dcg + 4, ci * 128:(ci + 1) * 128]

    def ln_chunk(src_ap, slot, ci):
        xb = fsl(XB + slot * 2048, 2048)
        xnt = fsl(XNT + slot * 2048, 2048)
        a = p.emit("gpsimd", "dma_start", out=xb, in_=src_ap,
                   waits=[xslot_free[slot], ("sA", const_sA)], inc="sA", incn=16)
        wA = [("sA", a)]
        p.emit("vector", "tensor_reduce", ST[:, 0:1], xb, axis=AX.X, op=OP.add,
               waits=wA)
        vdrain()
        p.emit("scalar", "activation", fsl(XSQ, 2048), xb, AF.Square,
               accum_out=ST[:, 1:2], waits=wA, inc="sD")
        wsq = [("sD", p.c("sD"))]
        p.emit("vector", "tensor_scalar", ST[:, 2:3], ST[:, 0:1], 1.0 / D, None,
               op0=OP.mult)
        p.emit("vector", "tensor_scalar", ST[:, 3:4], ST[:, 1:2], 1.0 / D, LN_EPS,
               op0=OP.mult, op1=OP.add, waits=wsq)
        vdrain()
        p.emit("vector", "tensor_tensor", ST[:, 4:5], ST[:, 2:3], ST[:, 2:3], op=OP.mult)
        vdrain()
        p.emit("vector", "tensor_tensor", ST[:, 5:6], ST[:, 3:4], ST[:, 4:5],
               op=OP.subtract, inc="sE")
        rsqrt_cols(5, [("sE", p.c("sE"))])
        vdrain()
        p.emit("vector", "tensor_tensor", ST[:, 12:13], ST[:, 2:3], ST[:, 11:12], op=OP.mult)
        vdrain()
        p.emit("vector", "tensor_scalar", ST[:, 13:14], ST[:, 12:13], -1.0, None,
               op0=OP.mult)
        p.emit("vector", "drain")
        e = p.emit("vector", "tensor_scalar", xnt, xb, ST[:, 11:12], ST[:, 13:14],
                   op0=OP.mult, op1=OP.add, waits=[xnt_free[slot]], inc="sE")
        xslot_free[slot] = ("sE", e)
        for g in range(4):
            tb = g % 2
            for d4 in range(4):
                w = []
                if d4 == 0:
                    w = [("sE", e), tbank_free[tb]]
                p.emit("tensor", "transpose", bank(tb)[:, d4 * 128:(d4 + 1) * 128],
                       xnt[:, (4 * g + d4) * 128:(4 * g + d4 + 1) * 128], IDN[:, :],
                       waits=w, inc="sC" if d4 == 3 else None)
            c = p.c("sC")
            dv = p.emit("scalar", "copy", xn_view(g, ci),
                        bank(tb).rearrange("p (a b) -> p a b", a=4),
                        waits=[("sC", c)], inc="sD")
            tbank_free[tb] = ("sD", dv)
            if g == 3:
                xnt_free[slot] = ("sC", c)

    for tile in range(8):
        for ci in range(4):
            ln_chunk(d_xs[(tile * 4 + ci) * 128:(tile * 4 + ci + 1) * 128, :],
                     (tile * 4 + ci) % 2, ci)
        xn_ready = p.c("sD")
        for fc in range(2):
            for dc in range(16):
                w = []
                if dc == 0:
                    w = [("sD", xn_ready), ("sB", const_sB), kbank_free[fc]]
                mm(bank(2 + fc), fsl(WKS + (fc * 16 + dc) * 128, 128),
                   fsl(XF + dc * 512, 512), dc == 0, dc == 15, waits=w,
                   inc="sC" if dc == 15 else None)
        kps = p.c("sC")
        for fc in range(2):
            p.emit("vector", "tensor_copy", fsl(KRW + fc * 512, 512), bank(2 + fc),
                   waits=[("sC", kps), krw_war] if fc == 0 else [], inc="sE")
        kraw_e = p.c("sE")
        kbank_free[0] = kbank_free[1] = ("sE", kraw_e)
        for fc in range(2):
            p.emit("scalar", "activation", fsl(KSQ + fc * 512, 512),
                   fsl(KRW + fc * 512, 512), AF.Square,
                   waits=[("sE", kraw_e)] if fc == 0 else [], inc="sD")
        ksq_d = p.c("sD")
        krw_war = ("sD", ksq_d)
        for fc in range(2):
            mm(bankrow(4), ONE[:, :], fsl(KSQ + fc * 512, 512), fc == 0, fc == 1,
               waits=[("sD", ksq_d), row4_free] if fc == 0 else [],
               inc="sC" if fc == 1 else None)
        ve = rsqrt_row(bankrow(4), 512, 1.0 / HD, RMS_EPS, [bc5_war], RWB1)
        row4_free = ("sE", ve)
        rrow_e = p.c("sE")
        bc = mm(bank(5), ONR[:, :], F[0:1, RWB1:RWB1 + 512], True, True,
                waits=[("sE", rrow_e), ("sE", const_sE)], inc="sC")
        bc5_war = ("sC", bc)
        ca = p.emit("gpsimd", "dma_start", out=fsl(CKS, 512),
                    in_=d_ck[:, tile * 512:(tile + 1) * 512],
                    waits=[cks_free], inc="sA", incn=16)
        ca = p.emit("gpsimd", "dma_start", out=fsl(SKS, 512),
                    in_=d_sk[:, tile * 512:(tile + 1) * 512], inc="sA", incn=16)
        for fc in range(2):
            p.emit("vector", "tensor_tensor", fsl(KSQ + fc * 512, 512),
                   fsl(KRW + fc * 512, 512), bank(5), op=OP.mult,
                   waits=[("sC", bc)] if fc == 0 else [])
            p.emit("vector", "tensor_scalar", fsl(KRW + fc * 512, 512),
                   fsl(KSQ + fc * 512, 512), GK[:, fc:fc + 1], None, op0=OP.mult)
        k0 = fsl(KRW, 512); k1 = fsl(KRW + 512, 512)
        t0 = fsl(KSQ, 512); t1 = fsl(KSQ + 512, 512)
        ck = fsl(CKS, 512); sk = fsl(SKS, 512)
        p.emit("vector", "tensor_tensor", t0, k0, ck, op=OP.mult, waits=[("sA", ca)])
        p.emit("vector", "tensor_tensor", t1, k1, sk, op=OP.mult)
        p.emit("vector", "tensor_tensor", fsl(KB + tile * 512, 512), t0, t1,
               op=OP.subtract)
        p.emit("vector", "tensor_tensor", t0, k0, sk, op=OP.mult)
        e = p.emit("vector", "tensor_tensor", t1, k1, ck, op=OP.mult, inc="sE")
        cks_free = ("sE", e)
        p.emit("vector", "tensor_tensor", fsl(KB + 4096 + tile * 512, 512), t0, t1,
               op=OP.add, inc="sE")
        for ci in range(4):
            vb = 6 + ci % 2
            for dc in range(16):
                w = []
                if dc == 0:
                    w = [("sD", xn_ready), ("sB", const_sB), vbank_free[ci % 2]]
                mm(bank(vb, 256), fsl(XF + dc * 512 + ci * 128, 128),
                   fsl(WVS + dc * 256, 256), dc == 0, dc == 15, waits=w,
                   inc="sC" if dc == 15 else None)
            dv = p.emit("scalar", "copy",
                        fsl(VB + (tile * 4 + ci) * 256, 256), bank(vb, 256),
                        waits=[("sC", p.c("sC"))], inc="sD")
            vbank_free[ci % 2] = ("sD", dv)

    for tile in range(2):
        for ci in range(4):
            tc = tile * 4 + ci
            ln_chunk(d_xq[tc * 128:(tc + 1) * 128, :], tc % 2, ci)
        rd = p.c("sD")
        for ci in range(4):
            tc = tile * 4 + ci
            v = fsl(XF, 8192).rearrange("p (dc t) -> p dc t", dc=16)
            p.emit("gpsimd", "dma_start", out=d_xnsp[tc],
                   in_=v[:, :, ci * 128:(ci + 1) * 128],
                   waits=[("sD", rd)] if ci == 0 else [], inc="sG", incn=16)
        p.emit("tensor", "nop", waits=[("sG", p.c("sG"))])

    p.barrier()

    if phases < 2:
        return _finish(nc, es, p, sems, dump, F, d_xnsp)
    p.emit("gpsimd", "dma_start", out=fsl(CQF, 1024), in_=d_cq[:, :], inc="sA", incn=16)
    p.emit("gpsimd", "dma_start", out=fsl(SQF, 1024), in_=d_sq[:, :], inc="sA", incn=16)
    cq_a = p.c("sA")
    wq_free = [("bar", 0), ("bar", 0)]
    qbank_free = [("sE", 0), ("sE", 0)]
    row2_free = ("sE", 0)
    bc3_war = ("sC", 0)
    qrw_war = ("sD", 0)

    for half in range(2):
        xa = 0
        for k4 in range(4):
            v = fsl(XNH, 8192).rearrange("p (dc t) -> p dc t", dc=16)
            xa = p.emit("gpsimd", "dma_start", out=v[:, :, k4 * 128:(k4 + 1) * 128],
                        in_=d_xnsp[half * 4 + k4],
                        waits=[("sC", p.c("sC"))] if k4 == 0 else [],
                        inc="sA", incn=16)
        for h in range(H):
            bslots = []
            for i in range(2):
                b = p.emit("sync", "dma_start",
                           out=fsl(WQS + i * 2048, 2048).rearrange(
                               "p (dc f) -> p dc f", dc=16),
                           in_=d_wq[2 * h + i], waits=[wq_free[i]],
                           inc="sB", incn=16)
                bslots.append(b)
            for i in range(2):
                for dc in range(16):
                    w = []
                    if dc == 0:
                        w = [("sB", bslots[i]), ("sA", xa), qbank_free[i]]
                    mm(bank(i), fsl(WQS + i * 2048 + dc * 128, 128),
                       fsl(XNH + dc * 512, 512), dc == 0, dc == 15, waits=w,
                       inc="sC" if dc == 15 else None)
                wq_free[i] = ("sC", p.c("sC"))
            qmm = p.c("sC")
            for i in range(2):
                p.emit("vector", "tensor_copy", fsl(QRW + i * 512, 512), bank(i),
                       waits=[("sC", qmm), qrw_war] if i == 0 else [], inc="sE")
            qr = p.c("sE")
            qbank_free[0] = qbank_free[1] = ("sE", qr)
            for i in range(2):
                p.emit("scalar", "activation", fsl(QS2 + i * 512, 512),
                       fsl(QRW + i * 512, 512), AF.Square,
                       waits=[("sE", qr)] if i == 0 else [], inc="sD")
            qsqd = p.c("sD")
            qrw_war = ("sD", qsqd)
            for i in range(2):
                mm(bankrow(2), ONE[:, :], fsl(QS2 + i * 512, 512), i == 0, i == 1,
                   waits=[("sD", qsqd), row2_free] if i == 0 else [],
                   inc="sC" if i == 1 else None)
            ve = rsqrt_row(bankrow(2), 512, 1.0 / HD, RMS_EPS, [bc3_war], RWB2)
            row2_free = ("sE", ve)
            bc = mm(bank(3), ONR[:, :], F[0:1, RWB2:RWB2 + 512], True, True,
                    waits=[("sE", p.c("sE"))], inc="sC")
            bc3_war = ("sC", bc)
            for i in range(2):
                fc = 2 * h + i
                p.emit("vector", "tensor_tensor", fsl(QS2 + i * 512, 512),
                       fsl(QRW + i * 512, 512), bank(3), op=OP.mult,
                       waits=[("sC", bc), ("sA", cq_a)] if i == 0 else [])
                p.emit("vector", "tensor_scalar", fsl(QRW + i * 512, 512),
                       fsl(QS2 + i * 512, 512), GQ[:, fc:fc + 1], None, op0=OP.mult)
            q0 = fsl(QRW, 512); q1 = fsl(QRW + 512, 512)
            t0 = fsl(QS2, 512); t1 = fsl(QS2 + 512, 512)
            cq = fsl(CQF + half * 512, 512); sq = fsl(SQF + half * 512, 512)
            o0 = fsl(QB + h * 1024 + half * 512, 512)
            o1 = fsl(QB + 8192 + h * 1024 + half * 512, 512)
            p.emit("vector", "tensor_tensor", t0, q0, cq, op=OP.mult)
            p.emit("vector", "tensor_tensor", t1, q1, sq, op=OP.mult)
            p.emit("vector", "tensor_tensor", o0, t0, t1, op=OP.subtract)
            p.emit("vector", "tensor_tensor", t0, q0, sq, op=OP.mult)
            p.emit("vector", "tensor_tensor", t1, q1, cq, op=OP.mult)
            p.emit("vector", "tensor_tensor", o1, t0, t1, op=OP.add, inc="sE")

    p.barrier()

    if phases < 3:
        return _finish(nc, es, p, sems, dump, F, d_xnsp)
    sbank_free = [("bar", 0), ("bar", 0)]
    probs_free = [("bar", 0), ("bar", 0)]
    cbank_free = ("bar", 0)
    row4_free = ("sE", 0)
    wo_free = [("bar", 0), ("bar", 0)]
    otb_free = [("bar", 0), ("bar", 0)]
    b5war = ("bar", 0)
    b5_reader = ("sD", 0)
    otb_prev_qc = ("bar", 0)

    for qc in range(NQC):
        kext = KEXT[qc]
        ma = p.emit("gpsimd", "dma_start",
                    out=fsl(MSK, 2048).rearrange("p (k q) -> p k q", k=8),
                    in_=d_mask[qc], waits=[("sE", p.c("sE"))], inc="sA", incn=16)
        xq_a = 0
        for i in range(2):
            xq_a = p.emit("gpsimd", "dma_start", out=fsl(XQT + i * 2048, 2048),
                          in_=d_xq[qc * 256 + i * 128: qc * 256 + (i + 1) * 128, :],
                          waits=[("sC", p.c("sC"))] if i == 0 else [],
                          inc="sA", incn=16)
        for hp in range(H // 2):
            for kc in range(kext):
                sb = kc % 2
                for fc in range(2):
                    w = []
                    if fc == 0:
                        w = [sbank_free[sb]]
                        if kc == 0:
                            w += [cbank_free, row4_free]
                    rhs = fsl(QB + fc * 8192 + 2 * hp * 1024, 2048).rearrange(
                        "p (h t) -> p h t", h=2)[:, :, qc * 256:(qc + 1) * 256]
                    mm(bank(sb, 512), fsl(KB + fc * 4096 + kc * 128, 128),
                       rhs, fc == 0, fc == 1, waits=w,
                       inc="sC" if fc == 1 else None)
                sc = p.c("sC")
                prb = fsl(PRB + sb * 512, 512)
                d = p.emit("scalar", "activation", prb, bank(sb, 512), AF.Exp,
                           scale=1.0 / 16.0,
                           waits=[("sC", sc), probs_free[sb]], inc="sD")
                sbank_free[sb] = ("sD", d)
                pw = ("sD", d)
                if kc >= kext - NMASK:
                    for hh in range(2):
                        e = p.emit("vector", "tensor_tensor",
                                   fsl(PRB + sb * 512 + hh * 256, 256),
                                   fsl(PRB + sb * 512 + hh * 256, 256),
                                   fsl(MSK + (kc - (kext - NMASK)) * 256, 256),
                                   op=OP.mult,
                                   waits=[("sD", d), ("sA", ma)] if hh == 0 else [],
                                   inc="sE" if hh == 1 else None)
                    pw = ("sE", e)
                last = kc == kext - 1
                mm(bank(2, 512), fsl(VB + kc * 256, 128), prb, kc == 0, last,
                   waits=[pw])
                mm(bank(3, 512), fsl(VB + kc * 256 + 128, 128), prb, kc == 0, last)
                pe = mm(bankrow(4, 512), ONE[:, :], prb, kc == 0, last, inc="sC")
                probs_free[sb] = ("sC", pe)
            re = p.emit("vector", "reciprocal", F[0:1, ROW3:ROW3 + 512],
                        bankrow(4, 512),
                        waits=[("sC", p.c("sC")), bc5_war], inc="sE")
            row4_free = ("sE", re)
            bc = mm(bank(5, 512), ONR[:, :], F[0:1, ROW3:ROW3 + 512], True, True,
                    waits=[("sE", re), b5war], inc="sC")
            bc5_war = ("sC", bc)
            d = p.emit("scalar", "copy", fsl(RBS, 512), bank(5, 512),
                       waits=[("sC", bc), cbank_free], inc="sD")
            b5_reader = ("sD", d)
            for i in range(2):
                for hh in range(2):
                    fc = 4 * hp + 2 * hh + i
                    p.emit("vector", "tensor_tensor",
                           fsl(CTX + fc * 256, 256),
                           bank(2 + i, 512)[:, hh * 256:(hh + 1) * 256],
                           fsl(RBS + hh * 256, 256), op=OP.mult,
                           waits=[("sD", d)] if (i == 0 and hh == 0) else [],
                           inc="sE" if (i == 1 and hh == 1) else None)
            cbank_free = ("sE", p.c("sE"))
        ctx_e = p.c("sE")
        first_mm_done = False
        for fc in range(16):
            slot = fc % 2
            b = p.emit("sync", "dma_start",
                       out=fsl(WOS + slot * 2048, 2048),
                       in_=d_wo[fc], waits=[wo_free[slot]], inc="sB", incn=16)
            for qsub in range(2):
                for og in range(4):
                    bk = qsub * 4 + og
                    w = []
                    if qsub == 0 and og == 0:
                        w = [("sB", b)]
                        if fc == 0:
                            w += [("sE", ctx_e), cbank_free, b5_reader,
                                  sbank_free[0], sbank_free[1], row4_free,
                                  otb_prev_qc]
                    mm(bank(bk, 512), fsl(CTX + fc * 256 + qsub * 128, 128),
                       fsl(WOS + slot * 2048 + og * 512, 512),
                       fc == 0, fc == 15, waits=w,
                       inc="sC" if (qsub == 1 and og == 3) else None)
            wo_free[slot] = ("sC", p.c("sC"))
        omm = p.c("sC")
        for qsub in range(2):
            for og in range(4):
                slot = (qsub * 4 + og) % 2
                w = [otb_free[slot]]
                if qsub == 0 and og == 0:
                    w += [("sC", omm), ("sA", xq_a)]
                e = p.emit("vector", "tensor_tensor", fsl(OTB + slot * 512, 512),
                           bank(qsub * 4 + og, 512),
                           fsl(XQT + qsub * 2048 + og * 512, 512), op=OP.add,
                           waits=w, inc="sE")
                g = p.emit("gpsimd", "dma_start",
                           out=d_out[qc * 256 + qsub * 128: qc * 256 + (qsub + 1) * 128,
                                     og * 512:(og + 1) * 512],
                           in_=fsl(OTB + slot * 512, 512), waits=[("sE", e)],
                           inc="sG", incn=16)
                otb_free[slot] = ("sG", g)
        allfree = ("sE", p.c("sE"))
        otb_prev_qc = allfree
        sbank_free = [allfree, allfree]
        cbank_free = allfree
        row4_free = allfree
        b5war = allfree

    p.barrier()

    if phases < 4:
        return _finish(nc, es, p, sems, dump, F, d_xnsp)
    store_before_p4 = p.c("sG")
    cP4 = p.c("sC"); eP4 = p.c("sE"); dP4 = p.c("sD")

    xnf4 = XNFa.rearrange("p (tc dc t) -> p tc dc t", tc=8, dc=16)
    xa = 0
    for tc in range(8):
        w = ([("sE", eP4), ("sC", cP4), ("sG", p.c("sG")), ("sD", dP4)]
             if tc == 0 else [])
        xa = p.emit("gpsimd", "dma_start", out=xnf4[:, tc],
                    in_=d_xnsp[tc], waits=w, inc="sA", incn=16)
    xnf_g = XNFa.rearrange("p (tc dc t) -> p dc tc t", tc=8, dc=16)
    for q in range(4):
        p.emit("scalar", "activation",
               XNHa[:, q * 4096:(q + 1) * 4096].rearrange(
                   "p (dc tc t) -> p dc tc t", dc=4, tc=8),
               xnf_g[:, q * 4:(q + 1) * 4], AF.Copy,
               waits=[("sA", xa)] if q == 0 else [],
               inc="sD" if q == 3 else None)
    xnh_d = p.c("sD")
    for q in range(4):
        p.emit("vector", "tensor_tensor",
               XNLa[:, q * 4096:(q + 1) * 4096].rearrange(
                   "p (dc tc t) -> p dc tc t", dc=4, tc=8),
               xnf_g[:, q * 4:(q + 1) * 4],
               XNHa[:, q * 4096:(q + 1) * 4096].rearrange(
                   "p (dc tc t) -> p dc tc t", dc=4, tc=8),
               op=OP.subtract, waits=[("sD", xnh_d)] if q == 0 else [],
               inc="sE" if q == 3 else None)
    xn_split_e = p.c("sE")

    wf_free = [("bar", 0)] * 3
    wr_free = [("bar", 0)] * NWR
    gu_bank_free = [("bar", 0)] * 8
    hq_free = [("bar", 0), ("bar", 0)]
    spill_g = ("bar", 0)
    slab_e = {}
    bank_stop = [0] * 8

    def gu_slab(c, dc):
        gs = c * 16 + dc
        wfs = gs % 3
        wrs = gs % NWR
        wdma = [wf_free[wfs]]
        if gs == 0:
            wdma += [("sC", cP4), ("sE", eP4)]
        b = p.emit("sync", "dma_start", out=WFa[:, wfs * 1024:(wfs + 1) * 1024],
                   in_=d_wgu[c, dc], waits=wdma, inc="sB", incn=16)
        hi = WRa[:, wrs * 2048:wrs * 2048 + 1024]
        lo = WRa[:, wrs * 2048 + 1024:wrs * 2048 + 2048]
        whl = [("sB", b), wr_free[wrs]]
        if gs == 0:
            whl.append(("sE", xn_split_e))
        p.emit("scalar", "activation", hi, WFa[:, wfs * 1024:(wfs + 1) * 1024],
               AF.Copy, waits=whl, inc="sD")
        e = p.emit("vector", "tensor_tensor", lo,
                   WFa[:, wfs * 1024:(wfs + 1) * 1024], hi, op=OP.subtract,
                   waits=[("sD", p.c("sD"))], inc="sE")
        wf_free[wfs] = ("sE", e)
        slab_e[(c, dc)] = e

    for dc in range(16):
        gu_slab(0, dc)
    for c in range(16):
        for t2 in range(2):
            for dc in range(16):
                wrs = (c * 16 + dc) % NWR
                whi = WRa[:, wrs * 2048:wrs * 2048 + 1024]
                wlo = WRa[:, wrs * 2048 + 1024:wrs * 2048 + 2048]
                xhi = XNHa[:, dc * 1024 + t2 * 512:dc * 1024 + t2 * 512 + 512]
                xlo = XNLa[:, dc * 1024 + t2 * 512:dc * 1024 + t2 * 512 + 512]
                for bk in range(8):
                    j, gu = bk // 2, bk % 2
                    lh_hi = whi[:, (gu * 4 + j) * 128:(gu * 4 + j + 1) * 128]
                    lh_lo = wlo[:, (gu * 4 + j) * 128:(gu * 4 + j + 1) * 128]
                    for term in range(3):
                        lh = lh_hi if term < 2 else lh_lo
                        rh = xlo if term == 1 else xhi
                        w = []
                        if dc == 0 and term == 0:
                            w.append(gu_bank_free[bk])
                        if bk == 0 and term == 0:
                            w.append(("sE", slab_e[(c, dc)]))
                        cnt = mm(bank(bk, 512), lh, rh,
                                 dc == 0 and term == 0,
                                 dc == 15 and term == 2, waits=w,
                                 inc="sC" if term == 2 and (bk == 7 or dc == 15)
                                 else None)
                        if dc == 15 and term == 2:
                            bank_stop[bk] = cnt
                if t2 == 1:
                    wr_free[wrs] = ("sC", p.c("sC"))
            for j in range(4):
                gb = bank(j * 2, 512)
                ub = bank(j * 2 + 1, 512)
                hq = HQa[:, (c % 2) * 4096 + j * 1024 + t2 * 512:(c % 2) * 4096 + j * 1024 + t2 * 512 + 512]
                wsl = [("sC", bank_stop[j * 2])]
                if t2 == 0 and j == 0:
                    wsl.append(hq_free[c % 2])
                d = p.emit("scalar", "activation", hq, gb, AF.Silu, waits=wsl,
                           inc="sD")
                e = p.emit("vector", "tensor_tensor", hq, hq, ub, op=OP.mult,
                           waits=[("sD", d), ("sC", bank_stop[j * 2 + 1])],
                           inc="sE")
                gu_bank_free[j * 2] = ("sD", d)
                gu_bank_free[j * 2 + 1] = ("sE", e)
        hall = p.c("sE")
        if c < 15:
            for dc in range(16):
                gu_slab(c + 1, dc)
        base = (c % 2) * 4096
        J = [HQa[:, base + j * 1024:base + (j + 1) * 1024] for j in range(4)]
        A, Bb, C, Dd, E = [SPa[:, i * 1024:(i + 1) * 1024] for i in range(5)]
        p.emit("scalar", "activation", A, J[0], AF.Abs,
               waits=[("sE", hall), spill_g])
        p.emit("scalar", "activation", Bb, J[1], AF.Abs)
        p.emit("scalar", "activation", C, J[2], AF.Abs)
        p.emit("scalar", "activation", Dd, J[3], AF.Abs, inc="sD")
        wd0 = [("sD", p.c("sD"))]
        p.emit("vector", "tensor_tensor", E, A, Bb, op=OP.max, waits=wd0)
        p.emit("vector", "tensor_tensor", A, A, Bb, op=OP.min)
        p.emit("vector", "tensor_tensor", Bb, C, Dd, op=OP.max)
        p.emit("vector", "tensor_tensor", C, C, Dd, op=OP.min)
        p.emit("vector", "tensor_tensor", E, E, Bb, op=OP.min)
        p.emit("vector", "tensor_tensor", A, A, C, op=OP.max)
        p.emit("vector", "tensor_tensor", E, E, A, op=OP.max, inc="sE")
        p.emit("scalar", "activation", A, J[0], AF.Abs,
               waits=[("sE", p.c("sE"))])
        p.emit("scalar", "activation", Bb, J[1], AF.Abs)
        p.emit("scalar", "activation", C, J[2], AF.Abs, inc="sD")
        wd1 = [("sD", p.c("sD"))]
        for j, src in enumerate((A, Bb, C, Dd)):
            p.emit("vector", "tensor_tensor", src, src, E, op=OP.is_ge,
                   waits=wd1 if j == 0 else [])
            p.emit("vector", "tensor_tensor", J[j], J[j], src, op=OP.mult,
                   inc="sE" if j == 3 else None)
        spars_e = p.c("sE")
        d = p.emit("scalar", "activation", HQHa[:, :], HQa[:, base:base + 4096],
                   AF.Copy, waits=[("sE", spars_e)], inc="sD")
        e = p.emit("vector", "tensor_tensor", HQLa[:, :], HQa[:, base:base + 4096],
                   HQHa[:, :], op=OP.subtract, waits=[("sD", d)],
                   inc="sE")
        hq_free[c % 2] = ("sE", e)
        hqh3 = HQHa.rearrange("p (j t) -> p j t", j=4)
        hql3 = HQLa.rearrange("p (j t) -> p j t", j=4)
        for sqt in range(4):
            p.emit("gpsimd", "dma_start", out=d_hspv[c][0][:, sqt],
                   in_=hqh3[:, :, sqt * 256:(sqt + 1) * 256],
                   waits=[("sD", d)] if sqt == 0 else [], inc="sG", incn=16)
            g2 = p.emit("gpsimd", "dma_start", out=d_hspv[c][1][:, sqt],
                        in_=hql3[:, :, sqt * 256:(sqt + 1) * 256],
                        waits=[("sE", e)] if sqt == 0 else [], inc="sG", incn=16)
        spill_g = ("sG", g2)

    gu_end = p.c("sC")
    spill_all = p.c("sG")
    eP4c = p.c("sE")
    dP4c = p.c("sD")
    wdf_free = [("bar", 0), ("bar", 0)]
    wdb_free = [("bar", 0)] * 4
    dbank_free = list(gu_bank_free)
    rbt_free = [("bar", 0)] * 8
    otf_free = [("bar", 0)] * 8
    dbank_stop = [0] * 8
    ic_cnt = [0] * 64
    hrd_war = [("bar", 0)] * 16
    ha = [None, None, None, None]

    def hrd_load(qt):
        d = {}
        for cq_ in range(16):
            w = [hrd_war[cq_]]
            if qt == 0 and cq_ == 0:
                w += [("sG", spill_all), ("sC", gu_end)]
            g = p.emit("gpsimd", "dma_start",
                       out=HRDa[:, cq_ * 2048:(cq_ + 1) * 2048].rearrange(
                           "p (j hl t) -> p j hl t", j=4, hl=2),
                       in_=d_hsp[qt, cq_],
                       waits=w, inc="sA", incn=16)
            for jq in range(4):
                d[jq * 16 + cq_] = g
        ha[qt] = d

    hrd_load(0)
    for qt in range(4):
        rbt_g = [0] * 8
        for bk in range(8):
            tch, dg = bk // 4, bk % 4
            w = [("sG", store_before_p4), rbt_free[bk]]
            if qt == 0 and bk == 0:
                w += [("sE", eP4c), ("sC", gu_end)]
            rbt_g[bk] = p.emit("gpsimd", "dma_start",
                               out=RBTa[:, bk * 512:(bk + 1) * 512],
                               in_=d_out[qt * 256 + tch * 128:
                                         qt * 256 + (tch + 1) * 128,
                                         dg * 512:(dg + 1) * 512],
                               waits=w, inc="sR", incn=16)
        for ic in range(64):
            wfs = ic % 2
            wbs = ic % 4
            w = [wdf_free[wfs]]
            if qt == 0 and ic == 0:
                w += [("sE", eP4c), ("sD", dP4c)]
            b = p.emit("sync", "dma_start", out=WDFa[:, wfs * 2048:(wfs + 1) * 2048],
                       in_=d_wd[ic], waits=w, inc="sB", incn=16)
            whh = WDBa[:, wbs * 4096:wbs * 4096 + 2048]
            wll = WDBa[:, wbs * 4096 + 2048:wbs * 4096 + 4096]
            whl = [("sB", b), wdb_free[wbs]]
            if qt == 0 and ic == 0:
                whl.append(("sG", spill_all))
            d = p.emit("scalar", "activation", whh,
                       WDFa[:, wfs * 2048:(wfs + 1) * 2048], AF.Copy, waits=whl,
                       inc="sD")
            e = p.emit("vector", "tensor_tensor", wll,
                       WDFa[:, wfs * 2048:(wfs + 1) * 2048], whh, op=OP.subtract,
                       waits=[("sD", p.c("sD"))], inc="sE")
            wdf_free[wfs] = ("sE", e)
            hco = (ic % 16) * 2048 + (ic // 16) * 512
            hh = HRDa[:, hco:hco + 256]
            hl = HRDa[:, hco + 256:hco + 512]
            for tch in range(2):
                for dg in range(4):
                    bk = tch * 4 + dg
                    for term in range(3):
                        lh = (hh if term < 2 else hl)[:, tch * 128:(tch + 1) * 128]
                        rh = (whh if term != 1 else wll)[:, dg * 512:(dg + 1) * 512]
                        w = []
                        if term == 0 and tch == 0 and dg == 0:
                            w = [("sE", e), ("sA", ha[qt][ic])]
                        if term == 0 and ic == 0:
                            w.append(dbank_free[bk])
                        cnt = mm(bank(bk, 512), lh, rh,
                                 ic == 0 and term == 0,
                                 ic == 63 and term == 2, waits=w,
                                 inc="sC" if term == 2 and (bk == 7 or ic == 63)
                                 else None)
                        if ic == 63 and term == 2:
                            dbank_stop[bk] = cnt
            ic_cnt[ic] = p.c("sC")
            wdb_free[wbs] = ("sC", p.c("sC"))
        dmm = p.c("sC")
        for cq_ in range(16):
            hrd_war[cq_] = ("sC", ic_cnt[48 + cq_])
        if qt < 3:
            hrd_load(qt + 1)
        for bk in range(8):
            tch, dg = bk // 4, bk % 4
            e2 = p.emit("vector", "tensor_tensor", OTFa[:, bk * 512:(bk + 1) * 512],
                        bank(bk, 512), RBTa[:, bk * 512:(bk + 1) * 512],
                        op=OP.add,
                        waits=[("sC", dbank_stop[bk]), ("sR", rbt_g[bk]),
                               otf_free[bk]], inc="sE")
            dbank_free[bk] = ("sE", e2)
            rbt_free[bk] = ("sE", e2)
            g2 = p.emit("gpsimd", "dma_start",
                        out=d_out[qt * 256 + tch * 128: qt * 256 + (tch + 1) * 128,
                                  dg * 512:(dg + 1) * 512],
                        in_=OTFa[:, bk * 512:(bk + 1) * 512], waits=[("sE", e2)],
                        inc="sG", incn=16)
            otf_free[bk] = ("sG", g2)

    p.barrier()
    return _finish(nc, es, p, sems, dump, F, d_xnsp)


def _finish(nc, es, p, sems, dump=False, F=None, d_xnsp=None):
    if dump:
        d_dbg = nc.dram_tensor("dbgF", [128, POOL], F32, kind="ExternalOutput").ap()
        d_dbg2 = nc.dram_tensor("dbgsp", [8, 128, D], F32, kind="ExternalOutput").ap()
        dbgb = es.enter_context(nc.sbuf_tensor("dbgb", [128, 256], F32)).ap()
        nch = (POOL + 4095) // 4096
        for i in range(nch):
            n = min(4096, POOL - i * 4096)
            p.emit("gpsimd", "dma_start", out=d_dbg[:, i*4096:i*4096+n],
                   in_=F[:, i*4096:i*4096+n], inc="sG", incn=16)
        for tc in range(8):
            for i in range(8):
                p.emit("gpsimd", "dma_start", out=dbgb[:, :],
                       in_=d_xnsp[tc][:, i*256:(i+1)*256],
                       waits=[("sG", p.c("sG"))], inc="sA", incn=16)
                p.emit("gpsimd", "dma_start", out=d_dbg2[tc][:, i*256:(i+1)*256],
                       in_=dbgb[:, :], waits=[("sA", p.c("sA"))], inc="sG", incn=16)
        p.emit("gpsimd", "nop", waits=[("sG", p.c("sG"))])
    with es, nc.Block() as block:
        def mk(engname):
            def body(eng):
                for op, args, kwargs, ws, inc, incn in p.ops[engname]:
                    for (sn, val) in ws:
                        eng.wait_ge(sems[sn], val)
                    inst = getattr(eng, op)(*args, **kwargs)
                    if inc is not None:
                        inst.then_inc(sems[inc], incn)
            return body
        block.tensor(mk("tensor"))
        block.vector(mk("vector"))
        block.scalar(mk("scalar"))
        block.gpsimd(mk("gpsimd"))
        block.sync(mk("sync"))
    return nc



def _core_inputs(c, x, shared, ckf_h, skf_h):
    seq, j = c // 4, c % 4
    chunks = [j, 7 - j, 8 + j, 15 - j]
    own_idx = np.concatenate([np.arange(ch * 256, (ch + 1) * 256) for ch in chunks])
    xs = np.ascontiguousarray(x[seq])
    xq = np.ascontiguousarray(x[seq][own_idx])
    cqf = np.ascontiguousarray(ckf_h[:, own_idx])
    sqf = np.ascontiguousarray(skf_h[:, own_idx])
    mask = np.zeros((NQC, 128, NMASK, 256), np.float32)
    for qc in range(NQC):
        base = KEXT[qc] * 128 - 1024
        kpos = base + np.arange(1024)
        qpos = own_idx[qc * 256:(qc + 1) * 256]
        m = (kpos[:, None] <= qpos[None, :]).astype(np.float32)
        mask[qc] = m.reshape(NMASK, 128, 256).transpose(1, 0, 2)
    d = {"xs": xs, "xq": xq, "ckf": ckf_h, "skf": skf_h,
         "cqf": cqf, "sqf": sqf, "mask": mask}
    d.update(shared)
    return d, own_idx


def prep_inputs(inputs):
    x = np.asarray(inputs["L0_attn_x"], np.float32)
    q_proj = np.asarray(inputs["L0_attn_q_proj"], np.float32)
    k_proj = np.asarray(inputs["L0_attn_k_proj"], np.float32)
    v_proj = np.asarray(inputs["L0_attn_v_proj"], np.float32)
    o_proj = np.asarray(inputs["L0_attn_o_proj"], np.float32)
    q_norm = np.asarray(inputs["L0_attn_q_norm"], np.float32)
    k_norm = np.asarray(inputs["L0_attn_k_norm"], np.float32)
    w_gate = np.asarray(inputs["L0_ffn_w_gate"], np.float32)
    w_up = np.asarray(inputs["L0_ffn_w_up"], np.float32)
    w_down = np.asarray(inputs["L0_ffn_w_down"], np.float32)

    wq_h = np.ascontiguousarray(q_proj.reshape(16, 128, 16, 128).transpose(0, 3, 2, 1))
    wk_h = np.ascontiguousarray(k_proj.reshape(2, 128, 16, 128).transpose(3, 0, 2, 1))
    wv_h = np.ascontiguousarray(v_proj.reshape(256, 16, 128).transpose(2, 1, 0))
    wo_h = np.ascontiguousarray(o_proj.T.reshape(16, 128, 2048))
    G5 = w_gate.reshape(16, 128, 16, 128, 4).transpose(2, 0, 1, 4, 3)
    U5 = w_up.reshape(16, 128, 16, 128, 4).transpose(2, 0, 1, 4, 3)
    wgu_h = np.ascontiguousarray(
        np.stack([G5, U5], axis=3).reshape(16, 16, 128, 1024))
    wd_h = np.ascontiguousarray(
        w_down.reshape(2048, 4, 2048).transpose(1, 0, 2).reshape(64, 128, 2048))
    gq_h = np.ascontiguousarray(np.tile(q_norm, 8).reshape(16, 128).T)
    gk_h = np.ascontiguousarray(k_norm.reshape(2, 128).T)

    idx = np.arange(128, dtype=np.float64)
    inv = 1.0 / (1e6 ** (2.0 * idx / 256.0))
    pos = np.arange(L, dtype=np.float64)
    fr = np.outer(inv, pos)
    ckf_h = np.cos(fr).astype(np.float32)
    skf_h = np.sin(fr).astype(np.float32)

    shared = {"wq": wq_h, "wk": wk_h, "wv": wv_h, "wo": wo_h, "wgu": wgu_h,
              "wd": wd_h, "gq": gq_h, "gk": gk_h,
              "ident": np.eye(128, dtype=np.float32)}

    in_maps, own_idxs = [], []
    for c in range(NCORE):
        m, oi = _core_inputs(c, x, shared, ckf_h, skf_h)
        in_maps.append(m)
        own_idxs.append(oi)
    return in_maps, own_idxs


def kernel(**inputs):
    in_maps, own_idxs = prep_inputs(inputs)
    nc = build_program()
    res = run_bass_kernel_spmd(nc, in_maps, core_ids=list(range(NCORE)))
    out = np.empty((B, L, D), np.float32)
    for c in range(NCORE):
        out[c // 4, own_idxs[c]] = np.asarray(res.results[c]["out"])
    return out
